# version 01
# baseline (speedup 1.0000x reference)
import math
import numpy as np
import ml_dtypes
from contextlib import ExitStack

import concourse.bass as bass
import concourse.mybir as mybir
from concourse.bass_utils import run_bass_kernel_spmd

F32 = mybir.dt.float32
BF16 = mybir.dt.bfloat16
AF = mybir.ActivationFunctionType
ALU = mybir.AluOpType

B = 2
S = 8192
D = 2048
HLOC = 4
DH = 128
II = 8192
TOK = 2048
NT = TOK // 128
NKC = D // 128
NIC = II // 128
SCALE = float(1.0 / math.sqrt(math.sqrt(DH)))
MASKVAL = -30000.0
ENGINES = ("sync", "gpsimd", "tensor", "vector", "scalar")


class Sem:
    def __init__(self, es, nc, name):
        self.h = es.enter_context(nc.semaphore(name))
        self.n = 0


class Prog:
    def __init__(self, nc):
        self.nc = nc
        self.ops = {e: [] for e in ENGINES}

    def emit(self, eng, fn):
        self.ops[eng].append(fn)

    def wait(self, eng, sem, v):
        if v and v > 0:
            self.emit(eng, lambda E, h=sem.h, v=v: E.wait_ge(h, v))

    def waits(self, eng, ws):
        for (s, v) in ws:
            self.wait(eng, s, v)

    def flush(self):
        ops = self.ops
        self.ops = {e: [] for e in ENGINES}
        todo = [(e, ops[e]) for e in ENGINES if ops[e]]
        if not todo:
            return
        with self.nc.Block() as block:
            for e, lst in todo:
                def runner(E, lst=lst):
                    for fn in lst:
                        fn(E)
                getattr(block, e)(runner)


def build(dbg=None):
    nc = bass.Bass("TRN2", target_bir_lowering=False, debug=False)
    p = Prog(nc)
    es = ExitStack()

    x_in = nc.dram_tensor("x", [TOK, D], F32, kind="ExternalInput").ap()
    wq = nc.dram_tensor("wq", [D, D], F32, kind="ExternalInput").ap()
    wk = nc.dram_tensor("wk", [D, D], F32, kind="ExternalInput").ap()
    wv = nc.dram_tensor("wv", [D, D], F32, kind="ExternalInput").ap()
    wo = nc.dram_tensor("wo", [D, D], F32, kind="ExternalInput").ap()
    bq2 = nc.dram_tensor("bq2", [128, NKC], F32, kind="ExternalInput").ap()
    bk2 = nc.dram_tensor("bk2", [128, NKC], F32, kind="ExternalInput").ap()
    bo2 = nc.dram_tensor("bo2", [128, NKC], F32, kind="ExternalInput").ap()
    bv_row = nc.dram_tensor("bv_row", [1, D], F32, kind="ExternalInput").ap()
    wg = nc.dram_tensor("wg", [D, II], F32, kind="ExternalInput").ap()
    wu = nc.dram_tensor("wu", [D, II], F32, kind="ExternalInput").ap()
    wd = nc.dram_tensor("wd", [II, D], F32, kind="ExternalInput").ap()
    out = nc.dram_tensor("out", [TOK, D], F32, kind="ExternalOutput").ap()

    wq_bf = nc.dram_tensor("wq_bf", [D, D], BF16).ap()
    wk_bf = nc.dram_tensor("wk_bf", [D, D], BF16).ap()
    wv_bf = nc.dram_tensor("wv_bf", [D, D], BF16).ap()
    wo_bf = nc.dram_tensor("wo_bf", [D, D], BF16).ap()
    wg_p = nc.dram_tensor("wg_p", [D, II], BF16).ap()
    wu_p = nc.dram_tensor("wu_p", [D, II], BF16).ap()
    wd_bf = nc.dram_tensor("wd_bf", [II, D], BF16).ap()
    kv_loc = nc.dram_tensor("kv_loc", [2 * D, TOK], BF16).ap()
    q_loc = nc.dram_tensor("q_loc", [D, TOK], BF16).ap()
    kv_g = nc.dram_tensor("kv_g", [8 * D, TOK], BF16).ap()
    q_g = nc.dram_tensor("q_g", [4 * D, TOK], BF16).ap()
    k_my = nc.dram_tensor("k_my", [HLOC * 64 * 128, 128], BF16).ap()
    v_my = nc.dram_tensor("v_my", [HLOC * 64 * 128, 128], BF16).ap()
    q_my = nc.dram_tensor("q_my", [HLOC * 128, S], BF16).ap()
    attn_loc = nc.dram_tensor("attn_loc", [HLOC * 128, S], BF16).ap()
    attn_g = nc.dram_tensor("attn_g", [D, S], BF16).ap()
    o_dram = nc.dram_tensor("o_dram", [D, TOK], BF16).ap()
    dn_dram = nc.dram_tensor("dn_dram", [D, TOK], BF16).ap()
    x2d = nc.dram_tensor("x2d", [TOK, D], F32).ap()
    hsT = nc.dram_tensor("hsT", [II, TOK], BF16).ap()

    ident_d = nc.inline_tensor(np.eye(128, dtype=ml_dtypes.bfloat16), "identc").ap()
    onescol_d = nc.inline_tensor(np.ones((128, 1), dtype=np.float32), "onescol").ap()
    onesrow_d = nc.inline_tensor(np.ones((1, 128), dtype=np.float32), "onesrow").ap()

    sP = Sem(es, nc, "sP")
    sV = Sem(es, nc, "sV")
    sS = Sem(es, nc, "sS")
    sGc = Sem(es, nc, "sGc")
    sCONST = Sem(es, nc, "sCONST")
    sCk = Sem(es, nc, "sCk"); sCv = Sem(es, nc, "sCv"); sCq = Sem(es, nc, "sCq")
    sCo = Sem(es, nc, "sCo"); sCd = Sem(es, nc, "sCd")
    sX = [Sem(es, nc, "sX%d" % i) for i in range(2)]
    sW = [Sem(es, nc, "sW%d" % i) for i in range(2)]
    sSt = [Sem(es, nc, "sSt%d" % i) for i in range(4)]
    sQs = [Sem(es, nc, "sQs%d" % i) for i in range(4)]
    sPw = [Sem(es, nc, "sPw%d" % i) for i in range(2)]
    sPst = [Sem(es, nc, "sPst%d" % i) for i in range(2)]
    sLOC = Sem(es, nc, "sLOC")
    sKb = [Sem(es, nc, "sKb%d" % i) for i in range(2)]
    sVb = [Sem(es, nc, "sVb%d" % i) for i in range(2)]
    sQb = [Sem(es, nc, "sQb%d" % i) for i in range(2)]
    sC1 = Sem(es, nc, "sC1"); sC2 = Sem(es, nc, "sC2"); sC3 = Sem(es, nc, "sC3")
    ALLSEMS = ([sP, sV, sS, sGc, sCONST, sCk, sCv, sCq, sCo, sCd, sLOC,
                sC1, sC2, sC3] + sX + sW + sSt + sQs + sPw + sPst + sKb + sVb + sQb)

    def barrier():
        for e in ENGINES:
            for s in ALLSEMS:
                p.wait(e, s, s.n)

    def sb(name, shape, dt=F32, stack=es):
        return stack.enter_context(nc.sbuf_tensor(name, shape, dt))

    def ps_(name, shape, dt=F32, stack=es):
        return stack.enter_context(nc.psum_tensor(name, shape, dt))

    def dma(eng, sem, out_ap, in_ap, waits=()):
        p.waits(eng, waits)
        def f(E, o=out_ap, i=in_ap, h=sem.h):
            E.dma_start(out=o, in_=i).then_inc(h, 16)
        p.emit(eng, f)
        sem.n += 16
        return sem.n

    def dmat(eng, sem, out_ap, in_ap, waits=()):
        p.waits(eng, waits)
        def f(E, o=out_ap, i=in_ap, h=sem.h):
            E.dma_start_transpose(o, i).then_inc(h, 16)
        p.emit(eng, f)
        sem.n += 16
        return sem.n

    def op(eng, fn, inc=None, waits=()):
        p.waits(eng, waits)
        if inc is not None:
            def g(E, fn=fn, h=inc.h):
                fn(E).then_inc(h, 1)
            p.emit(eng, g)
            inc.n += 1
            return inc.n
        p.emit(eng, lambda E, fn=fn: fn(E))
        return None

    dbg_outs = {}
    def dump_dram(name, src_ap, shape, dt=F32):
        o = nc.dram_tensor("dbg_" + name, list(shape), dt, kind="ExternalOutput").ap()
        dbg_outs[name] = "dbg_" + name
        rows = shape[0]
        step = max(1, min(rows, (1 << 21) // max(1, int(shape[1]))))
        r = 0
        while r < rows:
            rr = min(step, rows - r)
            dma("gpsimd", sCd, o[r:r + rr, :], src_ap[r:r + rr, :])
            r += rr

    ident_sb = sb("ident_sb", [128, 128], BF16)
    ones_col = sb("ones_col", [128, 1], F32)
    ones_row = sb("ones_row", [1, 128], F32)
    bq_sb = sb("bq_sb", [128, NKC], F32)
    bk_sb = sb("bk_sb", [128, NKC], F32)
    bo_sb = sb("bo_sb", [128, NKC], F32)
    bvrow_sb = sb("bvrow_sb", [1, D], F32)
    bv_bc = sb("bv_bc", [128, D], F32)
    masks = sb("masks", [128, 4 * 512], F32)

    es_ln = ExitStack()
    lnxT = sb("lnxT", [128, NKC * TOK], BF16, es_ln)

    dma("sync", sCONST, ident_sb[:, :], ident_d)
    dma("sync", sCONST, ones_col[:, :], onescol_d)
    dma("sync", sCONST, ones_row[:, :], onesrow_d)
    dma("sync", sCONST, bq_sb[:, :], bq2)
    dma("sync", sCONST, bk_sb[:, :], bk2)
    dma("sync", sCONST, bo_sb[:, :], bo2)
    ld_const = dma("sync", sCONST, bvrow_sb[:, :], bv_row)

    cast_mark = {}
    for nm, csem, (csrc, cdst) in (("wk", sCk, (wk, wk_bf)), ("wv", sCv, (wv, wv_bf)),
                                   ("wq", sCq, (wq, wq_bf)), ("wo", sCo, (wo, wo_bf))):
        for c in range(4):
            m = dma("gpsimd", csem, cdst[c * 512:(c + 1) * 512, :], csrc[c * 512:(c + 1) * 512, :])
        cast_mark[nm] = (csem, m)
    for c in range(16):
        m = dma("gpsimd", sCd, wd_bf[c * 512:(c + 1) * 512, :], wd[c * 512:(c + 1) * 512, :])
    cast_mark["wd"] = (sCd, m)

    def mkmask(E):
        last = None
        for d in range(4):
            E.memset(masks[:, d * 512:(d + 1) * 512], 0.0)
            last = E.affine_select(
                out=masks[:, d * 512:(d + 1) * 512], in_=masks[:, d * 512:(d + 1) * 512],
                compare_op=ALU.is_ge, fill=MASKVAL,
                base=-128 * d, pattern=[[1, 512]], channel_multiplier=-1)
        return last
    mask_done = op("gpsimd", mkmask, inc=sGc)

    es1 = ExitStack()
    xt = [sb("xt%d" % i, [128, D], F32, es1) for i in range(2)]
    ln_bf = [sb("lnbf%d" % i, [128, D], BF16, es1) for i in range(2)]
    stat = sb("stat", [128, 20], F32, es1)
    scr32 = sb("scr32", [128, D], F32, es1)
    scrbf = sb("scrbf", [128, D], BF16, es1)
    ptr_ps = [ps_("ptrps%d" % i, [128, 128], BF16, es1) for i in range(2)]
    ps_bv = ps_("psbv", [128, 512], F32, es1)

    bvev = None
    for blk in range(4):
        w = [(sCONST, ld_const)] if blk == 0 else [(sV, bvev)]
        bm = op("tensor", lambda E: E.matmul(
            ps_bv[:, :], ones_row[0:1, :], bvrow_sb[0:1, 0:512], start=True, stop=True),
            inc=sP, waits=w) if blk == 0 else op(
            "tensor", lambda E, blk=blk: E.matmul(
                ps_bv[:, :], ones_row[0:1, :], bvrow_sb[0:1, blk * 512:(blk + 1) * 512],
                start=True, stop=True), inc=sP, waits=w)
        bvev = op("vector", lambda E, blk=blk: E.tensor_copy(
            bv_bc[:, blk * 512:(blk + 1) * 512], ps_bv[:, :]), inc=sV, waits=[(sP, bm)])
    bvbc_done = bvev

    def emit_ln_chain(xtile, statc, lnout, ready):
        c_sum = statc[:, 0:1]; c_sq = statc[:, 1:2]; c_mean = statc[:, 2:3]
        c_m2 = statc[:, 3:4]; c_var = statc[:, 4:5]; c_lnv = statc[:, 5:6]
        c_rstd = statc[:, 6:7]; c_nmr = statc[:, 7:8]
        m_sum = op("scalar", lambda E: E.activation(scrbf[:, :], xtile[:, :], AF.Identity,
                                                   accum_out=c_sum), inc=sS, waits=ready)
        m_sq = op("vector", lambda E: E.scalar_tensor_tensor(
            out=scr32[:, :], in0=xtile[:, :], scalar=1.0, in1=xtile[:, :],
            op0=ALU.mult, op1=ALU.mult, accum_out=c_sq), inc=sV, waits=ready)
        m_mean = op("vector", lambda E: E.tensor_scalar(
            out=c_mean, in0=c_sum, scalar1=1.0 / D, scalar2=None, op0=ALU.mult),
            inc=sV, waits=[(sS, m_sum)])
        m_m2 = op("vector", lambda E: E.tensor_tensor(c_m2, c_mean, c_mean, ALU.mult),
                  inc=sV, waits=[(sV, m_mean)])
        c_ve = statc[:, 8:9]; c_lh = statc[:, 9:10]
        m_var = op("vector", lambda E: E.scalar_tensor_tensor(
            out=c_var, in0=c_sq, scalar=1.0 / D, in1=c_m2, op0=ALU.mult, op1=ALU.subtract),
            inc=sV, waits=[(sV, m_m2), (sV, m_sq)])
        m_ve = op("vector", lambda E: E.tensor_scalar(
            out=c_ve, in0=c_var, scalar1=1e-5, scalar2=None, op0=ALU.add),
            inc=sV, waits=[(sV, m_var)])
        m_lnv = op("scalar", lambda E: E.activation(c_lnv, c_ve, AF.Ln),
                   inc=sS, waits=[(sV, m_ve)])
        m_lh = op("vector", lambda E: E.tensor_scalar(
            out=c_lh, in0=c_lnv, scalar1=-0.5, scalar2=None, op0=ALU.mult),
            inc=sV, waits=[(sS, m_lnv)])
        m_rstd = op("scalar", lambda E: E.activation(c_rstd, c_lh, AF.Exp),
                    inc=sS, waits=[(sV, m_lh)])
        m_nmr = op("vector", lambda E: E.tensor_scalar(
            out=c_nmr, in0=c_mean, scalar1=c_rstd, scalar2=-1.0, op0=ALU.mult, op1=ALU.mult),
            inc=sV, waits=[(sS, m_rstd)])
        m_norm = op("scalar", lambda E: E.activation(
            lnout[:, :], xtile[:, :], AF.Identity, scale=c_rstd, bias=c_nmr),
            inc=sS, waits=[(sV, m_nmr)])
        return m_norm

    trwar = {0: None, 1: None}

    def emit_transposes(lnout, tt, norm_mark, dst, pses):
        last = None
        for c in range(NKC):
            pp = pses[c % 2]
            w = [(sS, norm_mark)]
            if trwar[c % 2] is not None:
                w.append(trwar[c % 2])
            mark = op("tensor", lambda E, pp=pp, c=c, lnout=lnout: E.transpose(
                pp[:, :], lnout[:, c * 128:(c + 1) * 128], ident_sb[:, :]),
                inc=sP, waits=w)
            em = op("vector", lambda E, pp=pp, c=c, tt=tt, dst=dst: E.tensor_copy(
                dst[:, c * TOK + tt * 128: c * TOK + (tt + 1) * 128], pp[:, :]),
                inc=sV, waits=[(sP, mark)])
            trwar[c % 2] = (sV, em)
            last = em
        return last

    ln_trans_done = {}
    for tt in range(NT):
        buf = tt % 2
        w = []
        if tt >= 2:
            w.append((sV, ln_trans_done[tt - 2]))
        xm = dma("sync", sX[buf], xt[buf][:, :], x_in[tt * 128:(tt + 1) * 128, :], waits=w)
        nm = emit_ln_chain(xt[buf], stat[:, buf * 10:buf * 10 + 10], ln_bf[buf], [(sX[buf], xm)])
        tm = emit_transposes(ln_bf[buf], tt, nm, lnxT, ptr_ps)
        ln_trans_done[tt] = tm
    ln1_done = ln_trans_done[NT - 1]

    if dbg == "ln1":
        d_ln = nc.dram_tensor("dbg_ln1", [128, NKC * TOK], BF16, kind="ExternalOutput").ap()
        dbg_outs["ln1"] = "dbg_ln1"
        dma("scalar", sSt[0], d_ln, lnxT[:, :], waits=[(sV, ln1_done)])
        barrier(); p.flush()
        return nc, dbg_outs

    barrier()
    p.flush()
    es1.close()

    es2 = ExitStack()
    wstrip = [sb("wstrip%d" % i, [128, 16 * 512], BF16, es2) for i in range(2)]
    evq = [sb("evq%d" % i, [128, 512], BF16, es2) for i in range(4)]
    wtmp = [sb("wtmp%d" % i, [128, II], BF16, es2) for i in range(2)]
    perm = [sb("perm%d" % i, [128, II], BF16, es2) for i in range(2)]
    psA = ps_("psA", [128, 4 * 512], F32, es2)
    psB = ps_("psB", [128, 4 * 512], F32, es2)

    wstrip_war = {0: None, 1: None}
    ps_war = {id(psA): None, id(psB): None}
    ev_war = {i: None for i in range(4)}

    def qk_proj(wsrc_bf, bias_sb, scale, dst_dram, cast_ready, st_sems):
        for dg in range(4):
            wbuf = wstrip[dg % 2]
            w = [cast_ready]
            if wstrip_war[dg % 2] is not None:
                w.append(wstrip_war[dg % 2])
            lm = 0
            for dk in range(NKC):
                lm = dmat("sync", sW[dg % 2], wbuf[:, dk * 512:(dk + 1) * 512],
                          wsrc_bf[dg * 512:(dg + 1) * 512, dk * 128:(dk + 1) * 128],
                          waits=w if dk == 0 else ())
            for dci in range(4):
                dc = dg * 4 + dci
                ps = psA if dc % 2 == 0 else psB
                wp = [(sW[dg % 2], lm)]
                if ps_war[id(ps)] is not None:
                    wp.append(ps_war[id(ps)])
                mmark = None
                for dk in range(NKC):
                    for tb in range(4):
                        last = (dk == NKC - 1 and tb == 3)
                        mmark = op("tensor", lambda E, dk=dk, tb=tb, dci=dci, ps=ps, wbuf=wbuf: E.matmul(
                            ps[:, tb * 512:(tb + 1) * 512],
                            wbuf[:, dk * 512 + dci * 128: dk * 512 + (dci + 1) * 128],
                            lnxT[:, dk * TOK + tb * 512: dk * TOK + (tb + 1) * 512],
                            start=(dk == 0), stop=(dk == NKC - 1)),
                            inc=sP if last else None,
                            waits=wp if (dk == 0 and tb == 0) else ())
                wstrip_war[dg % 2] = (sP, mmark)
                ev_last = None
                for tb in range(4):
                    eb = evq[tb]
                    wv_ = [(sP, mmark)]
                    if ev_war[tb] is not None:
                        wv_.append(ev_war[tb])
                    em = op("vector", lambda E, tb=tb, ps=ps, eb=eb, dc=dc: E.tensor_scalar(
                        out=eb[:, :], in0=ps[:, tb * 512:(tb + 1) * 512],
                        scalar1=bias_sb[:, dc:dc + 1], scalar2=scale,
                        op0=ALU.add, op1=ALU.mult), inc=sV, waits=wv_)
                    sm = dma("scalar", st_sems[tb], dst_dram[dc * 128:(dc + 1) * 128, tb * 512:(tb + 1) * 512],
                             eb[:, :], waits=[(sV, em)])
                    ev_war[tb] = (st_sems[tb], sm)
                    ev_last = em
                ps_war[id(ps)] = (sV, ev_last)

    qk_proj(wk_bf, bk_sb, SCALE, kv_loc[0:D, :], cast_mark["wk"], sSt)

    for vb in range(4):
        vstrip = wstrip[vb % 2]
        w = [cast_mark["wv"]]
        if wstrip_war[vb % 2] is not None:
            w.append(wstrip_war[vb % 2])
        lm = 0
        for dk in range(NKC):
            lm = dmat("sync", sW[vb % 2], vstrip[:, dk * 512:(dk + 1) * 512],
                      wv_bf[vb * 512:(vb + 1) * 512, dk * 128:(dk + 1) * 128],
                      waits=w if dk == 0 else ())
        for tt2 in range(8):
            ps = psA if tt2 % 2 == 0 else psB
            wp = [(sW[vb % 2], lm)]
            if ps_war[id(ps)] is not None:
                wp.append(ps_war[id(ps)])
            mmark = None
            for sub in range(2):
                tt = tt2 * 2 + sub
                for dk in range(NKC):
                    last = (sub == 1 and dk == NKC - 1)
                    mmark = op("tensor", lambda E, dk=dk, tt=tt, sub=sub, ps=ps, vstrip=vstrip: E.matmul(
                        ps[:, sub * 512:(sub + 1) * 512],
                        lnxT[:, dk * TOK + tt * 128: dk * TOK + (tt + 1) * 128],
                        vstrip[:, dk * 512:(dk + 1) * 512],
                        start=(dk == 0), stop=(dk == NKC - 1)),
                        inc=sP if last else None,
                        waits=wp if (sub == 0 and dk == 0) else ())
            wstrip_war[vb % 2] = (sP, mmark)
            ev_last = None
            for sub in range(2):
                tt = tt2 * 2 + sub
                eb = evq[sub]
                wv_ = [(sP, mmark), (sV, bvbc_done)]
                if ev_war[sub] is not None:
                    wv_.append(ev_war[sub])
                em = op("vector", lambda E, sub=sub, ps=ps, eb=eb, vb=vb: E.tensor_tensor(
                    eb[:, :], ps[:, sub * 512:(sub + 1) * 512],
                    bv_bc[:, vb * 512:(vb + 1) * 512], ALU.add), inc=sV, waits=wv_)
                sm = dma("scalar", sSt[sub], kv_loc[D + tt * 128: D + (tt + 1) * 128,
                                              vb * 512:(vb + 1) * 512],
                         eb[:, :], waits=[(sV, em)])
                ev_war[sub] = (sSt[sub], sm)
                ev_last = em
            ps_war[id(ps)] = (sV, ev_last)
    kv_store_marks = [(s, s.n) for s in sSt]

    op("gpsimd", lambda E: E.collective_compute(
        "AllGather", ALU.bypass, replica_groups=[[0, 1, 2, 3], [4, 5, 6, 7]],
        ins=[kv_loc], outs=[kv_g]), inc=sC1, waits=kv_store_marks)

    qk_proj(wq_bf, bq_sb, SCALE, q_loc, cast_mark["wq"], sQs)
    q_store_marks = [(s, s.n) for s in sQs]

    op("gpsimd", lambda E: E.collective_compute(
        "AllGather", ALU.bypass, replica_groups=[[0, 1, 2, 3], [4, 5, 6, 7]],
        ins=[q_loc], outs=[q_g]), inc=sC2, waits=q_store_marks)
    ag_q = sC2.n

    prep_war = {0: None, 1: None}
    for mi, (src, dst) in enumerate(((wg, wg_p), (wu, wu_p))):
        for dk in range(NKC):
            buf = (mi * NKC + dk) % 2
            w = []
            if prep_war[buf] is not None:
                w.append(prep_war[buf])
            lm = dma("gpsimd", sPw[buf], wtmp[buf][:, :], src[dk * 128:(dk + 1) * 128, :], waits=w)
            view = wtmp[buf][:, :].rearrange("p (g four) -> p g four", four=4)
            pm = None
            for j in range(4):
                pm = op("vector", lambda E, j=j, buf=buf, view=view: E.tensor_copy(
                    perm[buf][:, j * 2048:(j + 1) * 2048], view[:, :, j]),
                    inc=sV if j == 3 else None,
                    waits=[(sPw[buf], lm)] if j == 0 else ())
            sm = dma("scalar", sPst[buf], dst[dk * 128:(dk + 1) * 128, :], perm[buf][:, :],
                     waits=[(sV, pm)])
            prep_war[buf] = (sPst[buf], sm)
    ffnprep_stores = [(s, s.n) for s in sPst]

    if dbg == "qkv":
        barrier()
        dump_dram("kv_loc", kv_loc, [2 * D, TOK], BF16)
        dump_dram("q_loc", q_loc, [D, TOK], BF16)
        dump_dram("kv_g", kv_g, [8 * D, TOK], BF16)
        dump_dram("wg_p", wg_p, [D, II], BF16)
        dump_dram("wd_bf", wd_bf, [II, D], BF16)
        barrier(); p.flush()
        return nc, dbg_outs

    barrier()
    p.flush()
    es2.close()

    es3 = ExitStack()
    qbuf = [sb("qbuf%d" % i, [128, 1024], BF16, es3) for i in range(2)]
    kbuf = [sb("kbuf%d" % i, [128, 128], BF16, es3) for i in range(2)]
    vbuf = [sb("vbuf%d" % i, [128, 128], BF16, es3) for i in range(2)]
    pbuf = [sb("pbuf%d" % i, [128, 1024], BF16, es3) for i in range(3)]
    accs = sb("accs", [128, 1024], F32, es3)
    rrow = sb("rrow", [1, 1024], F32, es3)
    recbc = sb("recbc", [128, 1024], F32, es3)
    evat = [sb("evat%d" % i, [128, 512], BF16, es3) for i in range(2)]
    sc_ps = [ps_("scps%d" % i, [128, 1024], F32, es3) for i in range(2)]
    at_ps = [ps_("atps%d" % i, [128, 512], F32, es3) for i in range(2)]
    aux_ps = ps_("auxps", [128, 1024], F32, es3)

    I8 = mybir.dt.int8
    kv_g8 = kv_g.bitcast(I8).tensor
    q_g8 = q_g.bitcast(I8).tensor
    k_my8 = k_my.bitcast(I8).tensor
    v_my8 = v_my.bitcast(I8).tensor
    q_my8 = q_my.bitcast(I8).tensor

    def emit_localize_kq(E):
        pid = E.partition_id()
        gr = E.snap(pid % 4, min_val=0, max_val=3)
        with E.register("loc_off") as lo:
            last = None
            for hl in range(HLOC):
                for r4 in range(4):
                    o_base = (hl * 64 + r4 * 16) * 128 * 256
                    E.reg_mul(lo, gr, 4 * 128 * TOK * 2)
                    E.reg_add(lo, lo, ((r4 * 2 * D) * TOK + hl * 128 * TOK) * 2)
                    in_ap = bass.AP(kv_g8, lo, [[256, 16], [TOK * 2, 128], [1, 256]])
                    out_ap = bass.AP(k_my8, o_base, [[128 * 256, 16], [256, 128], [1, 256]])
                    last = E.dma_start(out=out_ap, in_=in_ap)
                    last.then_inc(sLOC.h, 16)
                E.reg_mul(lo, gr, 4 * 128 * TOK * 2)
                E.reg_add(lo, lo, hl * 128 * TOK * 2)
                in_q = bass.AP(q_g8, lo, [[D * TOK * 2, 4], [TOK * 2, 128], [1, TOK * 2]])
                out_q = bass.AP(q_my8, hl * 128 * S * 2, [[TOK * 2, 4], [S * 2, 128], [1, TOK * 2]])
                last = E.dma_start(out=out_q, in_=in_q)
                last.then_inc(sLOC.h, 16)
        return last

    def emit_localize_v(E):
        pid = E.partition_id()
        gr = E.snap(pid % 4, min_val=0, max_val=3)
        with E.register("locv_off") as lo:
            last = None
            for hl in range(HLOC):
                for r4 in range(4):
                    o_base = (hl * 64 + r4 * 16) * 128 * 256
                    E.reg_mul(lo, gr, 4 * 128 * 2)
                    E.reg_add(lo, lo, ((r4 * 2 * D + D) * TOK + hl * 128) * 2)
                    in_ap = bass.AP(kv_g8, lo, [[128 * TOK * 2, 16], [TOK * 2, 128], [1, 256]])
                    out_ap = bass.AP(v_my8, o_base, [[128 * 256, 16], [256, 128], [1, 256]])
                    last = E.dma_start(out=out_ap, in_=in_ap)
                    last.then_inc(sLOC.h, 16)
        return last
    p.wait("sync", sC1, 1)
    p.wait("sync", sC2, 1)
    p.emit("sync", emit_localize_kq)
    p.wait("scalar", sC1, 1)
    p.emit("scalar", emit_localize_v)
    sLOC.n += 16 * (HLOC * 4 * 2 + HLOC)
    localize_done = sLOC.n

    def kv_tile_ap(t_dram, hl, kt):
        return bass.AP(t_dram.tensor, (hl * 64 + kt) * 128 * 128, [[128, 128], [1, 128]])

    war = {
        "kbuf": [None, None], "vbuf": [None, None], "qbuf": [None, None],
        "pbuf_pe": [None, None, None], "pbuf_dve": [None, None, None],
        "scbank": [None, None], "atps": [None, None],
        "acc": None, "evat": [None, None], "aux": None,
    }
    first_attn = True
    for hl in range(HLOC):
        for pr in range(8):
            kn = 8 * pr + 8
            qa_last = 8 * pr + 3
            qpb = pr % 2
            wq_ = [(sLOC, localize_done)]
            if war["qbuf"][qpb] is not None:
                wq_.append(war["qbuf"][qpb])
            qld = dma("sync", sQb[qpb], qbuf[qpb][:, :],
                      bass.AP(q_my.tensor, (hl * 128) * S + pr * 1024, [[S, 128], [1, 1024]]),
                      waits=wq_)
            accw = [war["acc"]] if war["acc"] is not None else []
            accm = op("vector", lambda E: E.memset(accs[:, :], 0.0), inc=sV, waits=accw)
            summark = accm
            for k in range(kn):
                kb = k % 2
                pb = k % 3
                par = k % 2
                qa_act = k <= qa_last
                c0, c1 = (0, 1024) if qa_act else (512, 1024)
                wk_ = [(sLOC, localize_done)] if first_attn else []
                first_attn = False
                if war["kbuf"][kb] is not None:
                    wk_.append(war["kbuf"][kb])
                km = dma("sync", sKb[kb], kbuf[kb][:, :], kv_tile_ap(k_my, hl, k), waits=wk_)
                wv_ = []
                if war["vbuf"][kb] is not None:
                    wv_.append(war["vbuf"][kb])
                vm = dma("sync", sVb[kb], vbuf[kb][:, :], kv_tile_ap(v_my, hl, k), waits=wv_)
                wsc = [(sKb[kb], km), (sQb[qpb], qld)]
                if war["scbank"][par] is not None:
                    wsc.append(war["scbank"][par])
                if qa_act:
                    op("tensor", lambda E, kb=kb, par=par, qpb=qpb: E.matmul(
                        sc_ps[par][:, 0:512], kbuf[kb][:, :], qbuf[qpb][:, 0:512],
                        start=True, stop=True), waits=wsc)
                    wsc = ()
                sm = op("tensor", lambda E, kb=kb, par=par, qpb=qpb: E.matmul(
                    sc_ps[par][:, 512:1024], kbuf[kb][:, :], qbuf[qpb][:, 512:1024],
                    start=True, stop=True), inc=sP, waits=wsc)
                war["kbuf"][kb] = (sP, sm)
                war["qbuf"][qpb] = (sP, sm)
                dmask = None
                if qa_act and k >= 8 * pr:
                    d, lo = k - 8 * pr, 0
                elif not qa_act:
                    d, lo = k - 8 * pr - 4, 512
                else:
                    d = None
                if d is not None:
                    dmask = op("vector", lambda E, par=par, d=d, lo=lo: E.scalar_tensor_tensor(
                        out=sc_ps[par][:, lo:lo + 512], in0=sc_ps[par][:, lo:lo + 512],
                        scalar=1.0, in1=masks[:, d * 512:(d + 1) * 512],
                        op0=ALU.mult, op1=ALU.add), inc=sV,
                        waits=[(sP, sm), (sGc, mask_done)])
                wex = [(sP, sm)]
                if dmask is not None:
                    wex.append((sV, dmask))
                if war["pbuf_pe"][pb] is not None:
                    wex.append(war["pbuf_pe"][pb])
                if war["pbuf_dve"][pb] is not None:
                    wex.append(war["pbuf_dve"][pb])
                em = op("scalar", lambda E, par=par, pb=pb, c0=c0, c1=c1: E.activation(
                    pbuf[pb][:, c0:c1], sc_ps[par][:, c0:c1], AF.Exp), inc=sS, waits=wex)
                war["scbank"][par] = (sS, em)
                smv = op("vector", lambda E, pb=pb, c0=c0, c1=c1: E.tensor_tensor(
                    accs[:, c0:c1], accs[:, c0:c1], pbuf[pb][:, c0:c1], ALU.add),
                    inc=sV, waits=[(sS, em)])
                summark = smv
                war["pbuf_dve"][pb] = (sV, smv)
                wat = [(sS, em), (sVb[kb], vm)]
                if k == 0:
                    for i_ in range(2):
                        if war["atps"][i_] is not None:
                            wat.append(war["atps"][i_])
                if qa_act:
                    op("tensor", lambda E, kb=kb, pb=pb, k=k, pr=pr: E.matmul(
                        at_ps[0][:, :], vbuf[kb][:, :], pbuf[pb][:, 0:512],
                        start=(k == 0), stop=(k == 8 * pr + 3)), waits=wat)
                    wat = ()
                am = op("tensor", lambda E, kb=kb, pb=pb, k=k, kn=kn: E.matmul(
                    at_ps[1][:, :], vbuf[kb][:, :], pbuf[pb][:, 512:1024],
                    start=(k == 0), stop=(k == kn - 1)), inc=sP, waits=wat)
                war["vbuf"][kb] = (sP, am)
                war["pbuf_pe"][pb] = (sP, am)
            wones = [(sV, summark)]
            if war["aux"] is not None:
                wones.append(war["aux"])
            op("tensor", lambda E: E.matmul(
                aux_ps[0:1, 0:512], ones_col[:, :], accs[:, 0:512], start=True, stop=True),
                waits=wones)
            om = op("tensor", lambda E: E.matmul(
                aux_ps[0:1, 512:1024], ones_col[:, :], accs[:, 512:1024], start=True, stop=True),
                inc=sP)
            war["acc"] = (sP, om)
            rm = op("vector", lambda E: E.reciprocal(rrow[0:1, :], aux_ps[0:1, :]),
                    inc=sV, waits=[(sP, om)])
            op("tensor", lambda E: E.matmul(
                aux_ps[:, 0:512], ones_row[0:1, :], rrow[0:1, 0:512], start=True, stop=True),
                waits=[(sV, rm)])
            bm = op("tensor", lambda E: E.matmul(
                aux_ps[:, 512:1024], ones_row[0:1, :], rrow[0:1, 512:1024], start=True, stop=True),
                inc=sP)
            bcm = op("vector", lambda E: E.tensor_copy(recbc[:, :], aux_ps[:, :]),
                     inc=sV, waits=[(sP, bm)])
            war["aux"] = (sV, bcm)
            for qi in range(2):
                wev = [(sV, bcm), (sP, sP.n)]
                if war["evat"][qi] is not None:
                    wev.append(war["evat"][qi])
                evm = op("vector", lambda E, qi=qi: E.scalar_tensor_tensor(
                    out=evat[qi][:, :], in0=at_ps[qi][:, :], scalar=1.0,
                    in1=recbc[:, qi * 512:(qi + 1) * 512], op0=ALU.mult, op1=ALU.mult),
                    inc=sV, waits=wev)
                war["atps"][qi] = (sV, evm)
                qt = 2 * pr + qi
                smq = dma("scalar", sQs[qi], attn_loc[hl * 128:(hl + 1) * 128, qt * 512:(qt + 1) * 512],
                          evat[qi][:, :], waits=[(sV, evm)])
                war["evat"][qi] = (sQs[qi], smq)
                if qi == 1:
                    war["aux"] = (sV, evm)
    attn_store_marks = [(sQs[0], sQs[0].n), (sQs[1], sQs[1].n)]

    op("gpsimd", lambda E: E.collective_compute(
        "AllGather", ALU.bypass, replica_groups=[[0, 1, 2, 3], [4, 5, 6, 7]],
        ins=[attn_loc], outs=[attn_g]), inc=sC3, waits=attn_store_marks)
    ag_at = sC3.n

    if dbg == "attn":
        barrier()
        dump_dram("attn_loc", attn_loc, [HLOC * 128, S], BF16)
        dump_dram("attn_g", attn_g, [D, S], BF16)
        barrier(); p.flush()
        return nc, dbg_outs

    barrier()
    p.flush()
    es3.close()

    es4 = ExitStack()
    attnT = sb("attnT", [128, NKC * TOK], BF16, es4)
    wstripO = [sb("wstripO%d" % i, [128, 16 * 512], BF16, es4) for i in range(2)]
    evo = [sb("evo%d" % i, [128, 512], BF16, es4) for i in range(4)]
    psA4 = ps_("psA4", [128, 4 * 512], F32, es4)
    psB4 = ps_("psB4", [128, 4 * 512], F32, es4)

    I8 = mybir.dt.int8
    attn_g8 = attn_g.bitcast(I8).tensor
    attnT8 = attnT[:, :].bitcast(I8)

    def emit_atload(E):
        pid = E.partition_id()
        gr = E.snap(pid % 4, min_val=0, max_val=3)
        with E.register("at_off") as lo:
            last = None
            for fc in range(NKC):
                E.reg_mul(lo, gr, TOK * 2)
                E.reg_add(lo, lo, fc * 128 * S * 2)
                in_ap = bass.AP(attn_g8, lo, [[S * 2, 128], [1, TOK * 2]])
                out_ap = bass.AP(attnT8.tensor, attnT8.offset + fc * TOK * 2,
                                 [[NKC * TOK * 2, 128], [1, TOK * 2]])
                last = E.dma_start(out=out_ap, in_=in_ap)
                last.then_inc(sLOC.h, 16)
        return last
    p.wait("sync", sC3, ag_at)
    p.emit("sync", emit_atload)
    sLOC.n += 16 * NKC
    atload = sLOC.n

    wstrip_war = {0: None, 1: None}
    ps_war = {id(psA4): None, id(psB4): None}
    ev_war = {i: None for i in range(4)}
    for dg in range(4):
        wbuf = wstripO[dg % 2]
        w = [cast_mark["wo"]]
        if wstrip_war[dg % 2] is not None:
            w.append(wstrip_war[dg % 2])
        lm = 0
        for fc in range(NKC):
            lm = dmat("sync", sW[dg % 2], wbuf[:, fc * 512:(fc + 1) * 512],
                      wo_bf[dg * 512:(dg + 1) * 512, fc * 128:(fc + 1) * 128],
                      waits=w if fc == 0 else ())
        for dci in range(4):
            dc = dg * 4 + dci
            ps = psA4 if dc % 2 == 0 else psB4
            wp = [(sW[dg % 2], lm), (sLOC, atload)]
            if ps_war[id(ps)] is not None:
                wp.append(ps_war[id(ps)])
            mmark = None
            for fc in range(NKC):
                for tb in range(4):
                    last = (fc == NKC - 1 and tb == 3)
                    mmark = op("tensor", lambda E, fc=fc, tb=tb, dci=dci, ps=ps, wbuf=wbuf: E.matmul(
                        ps[:, tb * 512:(tb + 1) * 512],
                        wbuf[:, fc * 512 + dci * 128: fc * 512 + (dci + 1) * 128],
                        attnT[:, fc * TOK + tb * 512: fc * TOK + (tb + 1) * 512],
                        start=(fc == 0), stop=(fc == NKC - 1)),
                        inc=sP if last else None,
                        waits=wp if (fc == 0 and tb == 0) else ())
            wstrip_war[dg % 2] = (sP, mmark)
            ev_last = None
            for tb in range(4):
                eb = evo[tb]
                wv_ = [(sP, mmark)]
                if ev_war[tb] is not None:
                    wv_.append(ev_war[tb])
                em = op("vector", lambda E, tb=tb, ps=ps, eb=eb, dc=dc: E.tensor_scalar(
                    out=eb[:, :], in0=ps[:, tb * 512:(tb + 1) * 512],
                    scalar1=bo_sb[:, dc:dc + 1], scalar2=None, op0=ALU.add),
                    inc=sV, waits=wv_)
                sm = dma("scalar", sSt[tb], o_dram[dc * 128:(dc + 1) * 128, tb * 512:(tb + 1) * 512],
                         eb[:, :], waits=[(sV, em)])
                ev_war[tb] = (sSt[tb], sm)
                ev_last = em
            ps_war[id(ps)] = (sV, ev_last)
    o_store_marks = [(s, s.n) for s in sSt]

    es4.close()
    es4b = ExitStack()
    xt4 = [sb("xt4%d" % i, [128, D], F32, es4b) for i in range(2)]
    oT4 = [sb("oT4%d" % i, [128, D], BF16, es4b) for i in range(2)]
    x2t = [sb("x2t%d" % i, [128, D], F32, es4b) for i in range(2)]
    ln_bf4 = [sb("lnbf4%d" % i, [128, D], BF16, es4b) for i in range(2)]
    stat4 = sb("stat4", [128, 20], F32, es4b)
    scr324 = sb("scr324", [128, D], F32, es4b)
    scrbf4 = sb("scrbf4", [128, D], BF16, es4b)
    trp4 = [ps_("trp4%d" % i, [128, 128], BF16, es4b) for i in range(2)]

    def emit_ln_chain4(xtile, statc, lnout, ready):
        c_sum = statc[:, 0:1]; c_sq = statc[:, 1:2]; c_mean = statc[:, 2:3]
        c_m2 = statc[:, 3:4]; c_var = statc[:, 4:5]; c_lnv = statc[:, 5:6]
        c_rstd = statc[:, 6:7]; c_nmr = statc[:, 7:8]
        m_sum = op("scalar", lambda E: E.activation(scrbf4[:, :], xtile[:, :], AF.Identity,
                                                   accum_out=c_sum), inc=sS, waits=ready)
        m_sq = op("vector", lambda E: E.scalar_tensor_tensor(
            out=scr324[:, :], in0=xtile[:, :], scalar=1.0, in1=xtile[:, :],
            op0=ALU.mult, op1=ALU.mult, accum_out=c_sq), inc=sV, waits=ready)
        m_mean = op("vector", lambda E: E.tensor_scalar(
            out=c_mean, in0=c_sum, scalar1=1.0 / D, scalar2=None, op0=ALU.mult),
            inc=sV, waits=[(sS, m_sum)])
        m_m2 = op("vector", lambda E: E.tensor_tensor(c_m2, c_mean, c_mean, ALU.mult),
                  inc=sV, waits=[(sV, m_mean)])
        c_ve = statc[:, 8:9]; c_lh = statc[:, 9:10]
        m_var = op("vector", lambda E: E.scalar_tensor_tensor(
            out=c_var, in0=c_sq, scalar=1.0 / D, in1=c_m2, op0=ALU.mult, op1=ALU.subtract),
            inc=sV, waits=[(sV, m_m2), (sV, m_sq)])
        m_ve = op("vector", lambda E: E.tensor_scalar(
            out=c_ve, in0=c_var, scalar1=1e-5, scalar2=None, op0=ALU.add),
            inc=sV, waits=[(sV, m_var)])
        m_lnv = op("scalar", lambda E: E.activation(c_lnv, c_ve, AF.Ln),
                   inc=sS, waits=[(sV, m_ve)])
        m_lh = op("vector", lambda E: E.tensor_scalar(
            out=c_lh, in0=c_lnv, scalar1=-0.5, scalar2=None, op0=ALU.mult),
            inc=sV, waits=[(sS, m_lnv)])
        m_rstd = op("scalar", lambda E: E.activation(c_rstd, c_lh, AF.Exp),
                    inc=sS, waits=[(sV, m_lh)])
        m_nmr = op("vector", lambda E: E.tensor_scalar(
            out=c_nmr, in0=c_mean, scalar1=c_rstd, scalar2=-1.0, op0=ALU.mult, op1=ALU.mult),
            inc=sV, waits=[(sS, m_rstd)])
        m_norm = op("scalar", lambda E: E.activation(
            lnout[:, :], xtile[:, :], AF.Identity, scale=c_rstd, bias=c_nmr),
            inc=sS, waits=[(sV, m_nmr)])
        return m_norm

    trwar = {0: None, 1: None}
    prev4 = {}
    x2st_war = {0: None, 1: None}
    x2_stores = 0
    for tt in range(NT):
        buf = tt % 2
        w = list(o_store_marks) if tt < 2 else [prev4[tt - 2]]
        xm = dma("sync", sX[buf], xt4[buf][:, :], x_in[tt * 128:(tt + 1) * 128, :], waits=w)
        wo_ = list(o_store_marks)
        if tt >= 2:
            wo_.append(prev4[tt - 2])
        om = dmat("sync", sKb[buf], oT4[buf][:, :], o_dram[0:D, tt * 128:(tt + 1) * 128], waits=wo_)
        wx2 = [(sX[buf], xm), (sKb[buf], om)]
        if x2st_war[buf] is not None:
            wx2.append(x2st_war[buf])
        x2m = op("vector", lambda E, buf=buf: E.tensor_tensor(
            x2t[buf][:, :], xt4[buf][:, :], oT4[buf][:, :], ALU.add),
            inc=sV, waits=wx2)
        x2_stores = dma("scalar", sQs[buf], x2d[tt * 128:(tt + 1) * 128, :], x2t[buf][:, :],
                        waits=[(sV, x2m)])
        x2st_war[buf] = (sQs[buf], x2_stores)
        nm = emit_ln_chain4(x2t[buf], stat4[:, buf * 10:buf * 10 + 10], ln_bf4[buf], [(sV, x2m)])
        tm = emit_transposes(ln_bf4[buf], tt, nm, lnxT, trp4)
        prev4[tt] = (sV, tm)
    ln2_done = prev4[NT - 1][1]

    if dbg == "x2":
        barrier()
        dump_dram("x2d", x2d, [TOK, D], F32)
        dump_dram("o_dram", o_dram, [D, TOK], BF16)
        datt = nc.dram_tensor("dbg_attnT", [128, NKC * TOK], BF16, kind="ExternalOutput").ap()
        dma("gpsimd", sCd, datt, attnT[:, :])
        dump_dram("attn_g", attn_g, [D, S], BF16)
        barrier(); p.flush()
        return nc, dbg_outs

    barrier()
    p.flush()
    es4b.close()
    es4.close()

    es5 = ExitStack()
    gstat = [sb("gstat%d" % i, [128, 2048], BF16, es5) for i in range(2)]
    ustat = [sb("ustat%d" % i, [128, 2048], BF16, es5) for i in range(2)]
    silu_sb = sb("silu_sb", [128, TOK], BF16, es5)
    hq = [sb("hq%d" % i, [128, TOK], BF16, es5) for i in range(4)]
    tq = [sb("tq%d" % i, [128, TOK], BF16, es5) for i in range(4)]
    sp1 = sb("sp1", [128, TOK], BF16, es5)
    sp2 = sb("sp2", [128, TOK], BF16, es5)
    sp3 = sb("sp3", [128, TOK], BF16, es5)
    sp4 = sb("sp4", [128, TOK], BF16, es5)
    hsx = [sb("hsx%d" % i, [128, TOK], BF16, es5) for i in range(4)]
    gps = ps_("gps", [128, 4 * 512], F32, es5)
    ups = ps_("ups", [128, 4 * 512], F32, es5)

    wstat_war = {0: None, 1: None}
    hq_war = [None] * 4
    tq_war = [None] * 4
    hs_war = [None] * 4
    silu_war = None
    g_ev = None
    u_ev = None
    sp4_war = None
    for m in range(16):
        habs = [None] * 4
        for j in range(4):
            c = j * 16 + m
            bufp = (m * 4 + j) % 2
            w = list(ffnprep_stores)
            if wstat_war[bufp] is not None:
                w.append(wstat_war[bufp])
            in_g = bass.AP(wg_p.tensor, c * 128, [[II, 128], [128 * II, 16], [1, 128]])
            in_u = bass.AP(wu_p.tensor, c * 128, [[II, 128], [128 * II, 16], [1, 128]])
            gv = gstat[bufp][:, :].rearrange("p (dk f) -> p dk f", f=128)
            uv = ustat[bufp][:, :].rearrange("p (dk f) -> p dk f", f=128)
            dma("sync", sW[bufp], gv, in_g, waits=w)
            lm = dma("sync", sW[bufp], uv, in_u)
            wg_ = [(sW[bufp], lm)]
            if g_ev is not None:
                wg_.append((sS, g_ev))
            gm = None
            for dk in range(NKC):
                for tb in range(4):
                    last = (dk == NKC - 1 and tb == 3)
                    gm = op("tensor", lambda E, dk=dk, tb=tb, bufp=bufp: E.matmul(
                        gps[:, tb * 512:(tb + 1) * 512],
                        gstat[bufp][:, dk * 128:(dk + 1) * 128],
                        lnxT[:, dk * TOK + tb * 512: dk * TOK + (tb + 1) * 512],
                        start=(dk == 0), stop=(dk == NKC - 1)),
                        inc=sP if last else None,
                        waits=wg_ if (dk == 0 and tb == 0) else ())
            wu_ = []
            if u_ev is not None:
                wu_.append((sV, u_ev))
            um = None
            for dk in range(NKC):
                for tb in range(4):
                    last = (dk == NKC - 1 and tb == 3)
                    um = op("tensor", lambda E, dk=dk, tb=tb, bufp=bufp: E.matmul(
                        ups[:, tb * 512:(tb + 1) * 512],
                        ustat[bufp][:, dk * 128:(dk + 1) * 128],
                        lnxT[:, dk * TOK + tb * 512: dk * TOK + (tb + 1) * 512],
                        start=(dk == 0), stop=(dk == NKC - 1)),
                        inc=sP if last else None,
                        waits=wu_ if (dk == 0 and tb == 0) else ())
            wstat_war[bufp] = (sP, um)
            wsl = [(sP, gm)]
            if silu_war is not None:
                wsl.append(silu_war)
            g_ev = op("scalar", lambda E: E.activation(silu_sb[:, :], gps[:, :], AF.Silu),
                      inc=sS, waits=wsl)
            whp = [(sP, um), (sS, g_ev)]
            if hq_war[j] is not None:
                whp.append(hq_war[j])
            u_ev = op("vector", lambda E, j=j: E.scalar_tensor_tensor(
                out=hq[j][:, :], in0=ups[:, :], scalar=1.0, in1=silu_sb[:, :],
                op0=ALU.mult, op1=ALU.mult), inc=sV, waits=whp)
            silu_war = (sV, u_ev)
            wab = [(sV, u_ev)]
            if tq_war[j] is not None:
                wab.append(tq_war[j])
            habs[j] = op("scalar", lambda E, j=j: E.activation(
                tq[j][:, :], hq[j][:, :], AF.Abs), inc=sS, waits=wab)
        wsp = [(sS, habs[0]), (sS, habs[1]), (sS, habs[2]), (sS, habs[3])]
        op("vector", lambda E: E.tensor_tensor(sp1[:, :], tq[0][:, :], tq[1][:, :], ALU.max),
           waits=wsp)
        op("vector", lambda E: E.tensor_tensor(sp2[:, :], tq[0][:, :], tq[1][:, :], ALU.min))
        op("vector", lambda E: E.tensor_tensor(sp3[:, :], tq[2][:, :], tq[3][:, :], ALU.max))
        w1 = op("vector", lambda E: E.tensor_tensor(sp4[:, :], tq[2][:, :], tq[3][:, :], ALU.min),
                inc=sV)
        op("vector", lambda E: E.tensor_tensor(sp1[:, :], sp1[:, :], sp3[:, :], ALU.min),
           waits=[(sV, w1)])
        w2 = op("vector", lambda E: E.tensor_tensor(sp2[:, :], sp2[:, :], sp4[:, :], ALU.max),
                inc=sV)
        w3 = op("vector", lambda E: E.tensor_tensor(sp3[:, :], sp1[:, :], sp2[:, :], ALU.max),
                inc=sV, waits=[(sV, w2)])
        for j in range(4):
            c = j * 16 + m
            wj = [(sV, w3)]
            if hs_war[j] is not None:
                wj.append(hs_war[j])
            if sp4_war is not None:
                wj.append(sp4_war)
            w4 = op("vector", lambda E, j=j: E.tensor_tensor(
                sp4[:, :], tq[j][:, :], sp3[:, :], ALU.is_ge), inc=sV, waits=wj)
            w5 = op("vector", lambda E, j=j: E.tensor_tensor(
                hsx[j][:, :], sp4[:, :], hq[j][:, :], ALU.mult), inc=sV, waits=[(sV, w4)])
            sp4_war = (sV, w5)
            hq_war[j] = (sV, w5)
            tq_war[j] = (sV, w4)
            hs_war[j] = (sSt[j], dma("scalar", sSt[j], hsT[c * 128:(c + 1) * 128, :], hsx[j][:, :],
                                 waits=[(sV, w5)]))
    hsT_stores = [(s, s.n) for s in sSt]

    if dbg == "gateup":
        barrier()
        dump_dram("hsT", hsT, [II, TOK], BF16)
        barrier(); p.flush()
        return nc, dbg_outs

    barrier()
    p.flush()
    es5.close()
    es_ln.close()

    es6 = ExitStack()
    hblk = sb("hblk", [128, NIC * 1024], BF16, es6)
    wdt = [sb("wdt%d" % i, [128, 256], BF16, es6) for i in range(2)]
    evd = [sb("evd%d" % i, [128, 512], BF16, es6) for i in range(2)]
    dps = ps_("dps", [128, 4 * 512], F32, es6)
    xt6 = [sb("xt6%d" % i, [128, D], F32, es6) for i in range(2)]
    dT6 = [sb("dT6%d" % i, [128, D], BF16, es6) for i in range(2)]
    x3t = [sb("x3t%d" % i, [128, D], F32, es6) for i in range(2)]

    dn_stores = 0
    out_war = {0: None, 1: None}
    x3_war = {0: None, 1: None}
    blk_mm_done = None
    for b in range(2):
        hbl = 0
        for c in range(NIC):
            w = []
            if c == 0:
                w.extend(hsT_stores)
                if blk_mm_done is not None:
                    w.append(blk_mm_done)
            hbl = dma("sync", sLOC, hblk[:, c * 1024:(c + 1) * 1024],
                      hsT[c * 128:(c + 1) * 128, b * 1024:(b + 1) * 1024], waits=w)
        wdt_war = {0: None, 1: None}
        ps_ev6 = None
        ev6_war = {0: None, 1: None}
        for dcp in range(8):
            wdl = 0
            mmark = None
            for c in range(NIC):
                j, mm_ = c // 16, c % 16
                bufw = c % 2
                in_wd = bass.AP(wd_bf.tensor, (512 * mm_ + j) * D + dcp * 256,
                                [[4 * D, 128], [1, 256]])
                w = [cast_mark["wd"]] if (b == 0 and dcp == 0 and c < 2) else []
                if wdt_war[bufw] is not None:
                    w.append(wdt_war[bufw])
                wdl = dma("sync", sW[bufw], wdt[bufw][:, :], in_wd, waits=w)
                mmw = [(sW[bufw], wdl), (sLOC, hbl)]
                if c == 0 and ps_ev6 is not None:
                    mmw.append(ps_ev6)
                for dci in range(2):
                    for sub in range(2):
                        last = (dci == 1 and sub == 1)
                        mmark = op("tensor", lambda E, c=c, dci=dci, sub=sub, bufw=bufw: E.matmul(
                            dps[:, (dci * 2 + sub) * 512:(dci * 2 + sub + 1) * 512],
                            wdt[bufw][:, dci * 128:(dci + 1) * 128],
                            hblk[:, c * 1024 + sub * 512: c * 1024 + (sub + 1) * 512],
                            start=(c == 0), stop=(c == NIC - 1)),
                            inc=sP if last else None,
                            waits=mmw if (dci == 0 and sub == 0) else ())
                wdt_war[bufw] = (sP, mmark)
            blk_mm_done = (sP, mmark)
            ev_last = None
            for dci in range(2):
                dc = dcp * 2 + dci
                for sub in range(2):
                    eb = evd[sub]
                    wv_ = [(sP, mmark)]
                    if ev6_war[sub] is not None:
                        wv_.append(ev6_war[sub])
                    em = op("vector", lambda E, dci=dci, sub=sub, eb=eb: E.tensor_copy(
                        eb[:, :], dps[:, (dci * 2 + sub) * 512:(dci * 2 + sub + 1) * 512]),
                        inc=sV, waits=wv_)
                    sm6 = dma("scalar", sSt[sub], dn_dram[dc * 128:(dc + 1) * 128,
                                              b * 1024 + sub * 512: b * 1024 + (sub + 1) * 512],
                        eb[:, :], waits=[(sV, em)])
                    ev6_war[sub] = (sSt[sub], sm6)
                    ev_last = em
            ps_ev6 = (sV, ev_last)
        dn_stores = [(sSt[0], sSt[0].n), (sSt[1], sSt[1].n)]
        for ttl in range(8):
            tt = b * 8 + ttl
            buf = ttl % 2
            w = []
            if x3_war[buf] is not None:
                w.append(x3_war[buf])
            xm = dma("sync", sX[buf], xt6[buf][:, :], x2d[tt * 128:(tt + 1) * 128, :], waits=w)
            dm_ = dmat("sync", sKb[buf], dT6[buf][:, :],
                       dn_dram[0:D, tt * 128: tt * 128 + 128],
                       waits=list(dn_stores))
            wx3 = [(sX[buf], xm), (sKb[buf], dm_)]
            if out_war[buf] is not None:
                wx3.append(out_war[buf])
            x3m = op("vector", lambda E, buf=buf: E.tensor_tensor(
                x3t[buf][:, :], xt6[buf][:, :], dT6[buf][:, :], ALU.add),
                inc=sV, waits=wx3)
            x3_war[buf] = (sV, x3m)
            om_ = dma("scalar", sSt[2 + buf], out[tt * 128:(tt + 1) * 128, :], x3t[buf][:, :],
                      waits=[(sV, x3m)])
            out_war[buf] = (sSt[2 + buf], om_)

    barrier()
    p.flush()
    es6.close()
    es.close()
    return nc, dbg_outs



def _core_inputs(inputs, core):
    X = inputs["L0_attn_X"]
    b, g = core // 4, core % 4
    xs = np.ascontiguousarray(X[b, g * TOK:(g + 1) * TOK, :], dtype=np.float32)
    def b2(v):
        return np.ascontiguousarray(np.asarray(v, np.float32).reshape(NKC, 128).T)
    return {
        "x": xs,
        "wq": np.ascontiguousarray(inputs["L0_attn_Wq"], np.float32),
        "wk": np.ascontiguousarray(inputs["L0_attn_Wk"], np.float32),
        "wv": np.ascontiguousarray(inputs["L0_attn_Wv"], np.float32),
        "wo": np.ascontiguousarray(inputs["L0_attn_Wo"], np.float32),
        "bq2": b2(inputs["L0_attn_bq"]),
        "bk2": b2(inputs["L0_attn_bk"]),
        "bo2": b2(inputs["L0_attn_bo"]),
        "bv_row": np.ascontiguousarray(np.asarray(inputs["L0_attn_bv"], np.float32).reshape(1, D)),
        "wg": np.ascontiguousarray(inputs["L0_ffn_w_gate"], np.float32),
        "wu": np.ascontiguousarray(inputs["L0_ffn_w_up"], np.float32),
        "wd": np.ascontiguousarray(inputs["L0_ffn_w_down"], np.float32),
    }


def kernel(**inputs):
    nc, _ = build()
    in_maps = [_core_inputs(inputs, c) for c in range(8)]
    res = run_bass_kernel_spmd(nc, in_maps, core_ids=list(range(8)))
    Y = np.empty((B, S, D), np.float32)
    for c in range(8):
        b, g = c // 4, c % 4
        Y[b, g * TOK:(g + 1) * TOK, :] = res.results[c]["out"]
    return Y


# version 03
# speedup vs baseline: 1.2510x; 1.2510x over previous
import math
import numpy as np
import ml_dtypes
from contextlib import ExitStack

import concourse.bass as bass
import concourse.mybir as mybir
from concourse.bass_utils import run_bass_kernel_spmd

F32 = mybir.dt.float32
BF16 = mybir.dt.bfloat16
AF = mybir.ActivationFunctionType
ALU = mybir.AluOpType

B = 2
S = 8192
D = 2048
HLOC = 4
DH = 128
II = 8192
TOK = 2048
NT = TOK // 128
NKC = D // 128
NIC = II // 128
SCALE = float(1.0 / math.sqrt(math.sqrt(DH)))
MASKVAL = -30000.0
ENGINES = ("sync", "gpsimd", "tensor", "vector", "scalar")


class Sem:
    def __init__(self, es, nc, name):
        self.h = es.enter_context(nc.semaphore(name))
        self.n = 0


class Prog:
    def __init__(self, nc):
        self.nc = nc
        self.ops = {e: [] for e in ENGINES}

    def emit(self, eng, fn):
        self.ops[eng].append(fn)

    def wait(self, eng, sem, v):
        if v and v > 0:
            self.emit(eng, lambda E, h=sem.h, v=v: E.wait_ge(h, v))

    def waits(self, eng, ws):
        for (s, v) in ws:
            self.wait(eng, s, v)

    def flush(self):
        ops = self.ops
        self.ops = {e: [] for e in ENGINES}
        todo = [(e, ops[e]) for e in ENGINES if ops[e]]
        if not todo:
            return
        with self.nc.Block() as block:
            for e, lst in todo:
                def runner(E, lst=lst):
                    for fn in lst:
                        fn(E)
                getattr(block, e)(runner)


def build(dbg=None):
    nc = bass.Bass("TRN2", target_bir_lowering=False, debug=False)
    p = Prog(nc)
    es = ExitStack()

    x_in = nc.dram_tensor("x", [TOK, D], F32, kind="ExternalInput").ap()
    wq = nc.dram_tensor("wq", [D, D], F32, kind="ExternalInput").ap()
    wk = nc.dram_tensor("wk", [D, D], F32, kind="ExternalInput").ap()
    wv = nc.dram_tensor("wv", [D, D], F32, kind="ExternalInput").ap()
    wo = nc.dram_tensor("wo", [D, D], F32, kind="ExternalInput").ap()
    bq2 = nc.dram_tensor("bq2", [128, NKC], F32, kind="ExternalInput").ap()
    bk2 = nc.dram_tensor("bk2", [128, NKC], F32, kind="ExternalInput").ap()
    bo2 = nc.dram_tensor("bo2", [128, NKC], F32, kind="ExternalInput").ap()
    bv_row = nc.dram_tensor("bv_row", [1, D], F32, kind="ExternalInput").ap()
    wg = nc.dram_tensor("wg", [D, II], F32, kind="ExternalInput").ap()
    wu = nc.dram_tensor("wu", [D, II], F32, kind="ExternalInput").ap()
    wd = nc.dram_tensor("wd", [II, D], F32, kind="ExternalInput").ap()
    out = nc.dram_tensor("out", [TOK, D], F32, kind="ExternalOutput").ap()

    wq_bf = nc.dram_tensor("wq_bf", [D, D], BF16).ap()
    wk_bf = nc.dram_tensor("wk_bf", [D, D], BF16).ap()
    wv_bf = nc.dram_tensor("wv_bf", [D, D], BF16).ap()
    wo_bf = nc.dram_tensor("wo_bf", [D, D], BF16).ap()
    wg_p = nc.dram_tensor("wg_p", [D, II], BF16).ap()
    wu_p = nc.dram_tensor("wu_p", [D, II], BF16).ap()
    wd_bf = nc.dram_tensor("wd_bf", [II, D], BF16).ap()
    kv_loc = nc.dram_tensor("kv_loc", [2 * D, TOK], BF16).ap()
    q_loc = nc.dram_tensor("q_loc", [D, TOK], BF16).ap()
    kv_g = nc.dram_tensor("kv_g", [8 * D, TOK], BF16).ap()
    q_g = nc.dram_tensor("q_g", [4 * D, TOK], BF16).ap()
    k_my = nc.dram_tensor("k_my", [HLOC * 64 * 128, 128], BF16).ap()
    v_my = nc.dram_tensor("v_my", [HLOC * 64 * 128, 128], BF16).ap()
    q_my = nc.dram_tensor("q_my", [HLOC * 128, S], BF16).ap()
    attn_loc = nc.dram_tensor("attn_loc", [HLOC * 128, S], BF16).ap()
    attn_g = nc.dram_tensor("attn_g", [D, S], BF16).ap()
    o_dram = nc.dram_tensor("o_dram", [D, TOK], BF16).ap()
    dn_dram = nc.dram_tensor("dn_dram", [D, TOK], BF16).ap()
    x2d = nc.dram_tensor("x2d", [TOK, D], F32).ap()
    hsT = nc.dram_tensor("hsT", [II, TOK], BF16).ap()

    ident_d = nc.inline_tensor(np.eye(128, dtype=ml_dtypes.bfloat16), "identc").ap()
    onescol_d = nc.inline_tensor(np.ones((128, 1), dtype=np.float32), "onescol").ap()
    onesrow_d = nc.inline_tensor(np.ones((1, 128), dtype=np.float32), "onesrow").ap()

    sP = Sem(es, nc, "sP")
    sV = Sem(es, nc, "sV")
    sS = Sem(es, nc, "sS")
    sGc = Sem(es, nc, "sGc")
    sCONST = Sem(es, nc, "sCONST")
    sCk = Sem(es, nc, "sCk"); sCv = Sem(es, nc, "sCv"); sCq = Sem(es, nc, "sCq")
    sCo = Sem(es, nc, "sCo"); sCd = Sem(es, nc, "sCd")
    sX = [Sem(es, nc, "sX%d" % i) for i in range(2)]
    sW = [Sem(es, nc, "sW%d" % i) for i in range(2)]
    sSt = [Sem(es, nc, "sSt%d" % i) for i in range(4)]
    sQs = [Sem(es, nc, "sQs%d" % i) for i in range(4)]
    sPw = [Sem(es, nc, "sPw%d" % i) for i in range(2)]
    sWd = [Sem(es, nc, "sWd%d" % i) for i in range(4)]
    sPst = [Sem(es, nc, "sPst%d" % i) for i in range(2)]
    sLOC = Sem(es, nc, "sLOC")
    sKb = [Sem(es, nc, "sKb%d" % i) for i in range(4)]
    sVb = [Sem(es, nc, "sVb%d" % i) for i in range(4)]
    sQb = [Sem(es, nc, "sQb%d" % i) for i in range(2)]
    sC1 = Sem(es, nc, "sC1"); sC2 = Sem(es, nc, "sC2"); sC3 = Sem(es, nc, "sC3")
    ALLSEMS = ([sP, sV, sS, sGc, sCONST, sCk, sCv, sCq, sCo, sCd, sLOC,
                sC1, sC2, sC3] + sX + sW + sSt + sQs + sPw + sPst + sKb + sVb + sQb + sWd)

    def barrier():
        for e in ENGINES:
            for s in ALLSEMS:
                p.wait(e, s, s.n)

    def sb(name, shape, dt=F32, stack=es):
        return stack.enter_context(nc.sbuf_tensor(name, shape, dt))

    def ps_(name, shape, dt=F32, stack=es):
        return stack.enter_context(nc.psum_tensor(name, shape, dt))

    def dma(eng, sem, out_ap, in_ap, waits=()):
        p.waits(eng, waits)
        def f(E, o=out_ap, i=in_ap, h=sem.h):
            E.dma_start(out=o, in_=i).then_inc(h, 16)
        p.emit(eng, f)
        sem.n += 16
        return sem.n

    def dmat(eng, sem, out_ap, in_ap, waits=()):
        p.waits(eng, waits)
        def f(E, o=out_ap, i=in_ap, h=sem.h):
            E.dma_start_transpose(o, i).then_inc(h, 16)
        p.emit(eng, f)
        sem.n += 16
        return sem.n

    def op(eng, fn, inc=None, waits=()):
        p.waits(eng, waits)
        if inc is not None:
            def g(E, fn=fn, h=inc.h):
                fn(E).then_inc(h, 1)
            p.emit(eng, g)
            inc.n += 1
            return inc.n
        p.emit(eng, lambda E, fn=fn: fn(E))
        return None

    dbg_outs = {}
    def dump_dram(name, src_ap, shape, dt=F32):
        o = nc.dram_tensor("dbg_" + name, list(shape), dt, kind="ExternalOutput").ap()
        dbg_outs[name] = "dbg_" + name
        rows = shape[0]
        step = max(1, min(rows, (1 << 21) // max(1, int(shape[1]))))
        r = 0
        while r < rows:
            rr = min(step, rows - r)
            dma("gpsimd", sCd, o[r:r + rr, :], src_ap[r:r + rr, :])
            r += rr

    ident_sb = sb("ident_sb", [128, 128], BF16)
    ones_col = sb("ones_col", [128, 1], F32)
    ones_row = sb("ones_row", [1, 128], F32)
    bq_sb = sb("bq_sb", [128, NKC], F32)
    bk_sb = sb("bk_sb", [128, NKC], F32)
    bo_sb = sb("bo_sb", [128, NKC], F32)
    bvrow_sb = sb("bvrow_sb", [1, D], F32)
    bv_bc = sb("bv_bc", [128, D], F32)
    masks = sb("masks", [128, 4 * 512], F32)

    es_ln = ExitStack()
    lnxT = sb("lnxT", [128, NKC * TOK], BF16, es_ln)

    dma("sync", sCONST, ident_sb[:, :], ident_d)
    dma("sync", sCONST, ones_col[:, :], onescol_d)
    dma("sync", sCONST, ones_row[:, :], onesrow_d)
    dma("sync", sCONST, bq_sb[:, :], bq2)
    dma("sync", sCONST, bk_sb[:, :], bk2)
    dma("sync", sCONST, bo_sb[:, :], bo2)
    ld_const = dma("sync", sCONST, bvrow_sb[:, :], bv_row)

    cast_mark = {}
    for nm, csem, (csrc, cdst) in (("wk", sCk, (wk, wk_bf)), ("wv", sCv, (wv, wv_bf)),
                                   ("wq", sCq, (wq, wq_bf)), ("wo", sCo, (wo, wo_bf))):
        for c in range(4):
            m = dma("gpsimd", csem, cdst[c * 512:(c + 1) * 512, :], csrc[c * 512:(c + 1) * 512, :])
        cast_mark[nm] = (csem, m)
    for c in range(16):
        m = dma("gpsimd", sCd, wd_bf[c * 512:(c + 1) * 512, :], wd[c * 512:(c + 1) * 512, :])
    cast_mark["wd"] = (sCd, m)

    def mkmask(E):
        last = None
        for d in range(4):
            E.memset(masks[:, d * 512:(d + 1) * 512], 0.0)
            last = E.affine_select(
                out=masks[:, d * 512:(d + 1) * 512], in_=masks[:, d * 512:(d + 1) * 512],
                compare_op=ALU.is_ge, fill=MASKVAL,
                base=-128 * d, pattern=[[1, 512]], channel_multiplier=-1)
        return last
    mask_done = op("gpsimd", mkmask, inc=sGc)

    es1 = ExitStack()
    xt = [sb("xt%d" % i, [128, D], F32, es1) for i in range(2)]
    ln_bf = [sb("lnbf%d" % i, [128, D], BF16, es1) for i in range(2)]
    stat = sb("stat", [128, 20], F32, es1)
    scr32 = sb("scr32", [128, D], F32, es1)
    scrbf = sb("scrbf", [128, D], BF16, es1)
    ptr_ps = [ps_("ptrps%d" % i, [128, 128], BF16, es1) for i in range(2)]
    ps_bv = ps_("psbv", [128, 512], F32, es1)

    bvev = None
    for blk in range(4):
        w = [(sCONST, ld_const)] if blk == 0 else [(sV, bvev)]
        bm = op("tensor", lambda E: E.matmul(
            ps_bv[:, :], ones_row[0:1, :], bvrow_sb[0:1, 0:512], start=True, stop=True),
            inc=sP, waits=w) if blk == 0 else op(
            "tensor", lambda E, blk=blk: E.matmul(
                ps_bv[:, :], ones_row[0:1, :], bvrow_sb[0:1, blk * 512:(blk + 1) * 512],
                start=True, stop=True), inc=sP, waits=w)
        bvev = op("vector", lambda E, blk=blk: E.tensor_copy(
            bv_bc[:, blk * 512:(blk + 1) * 512], ps_bv[:, :]), inc=sV, waits=[(sP, bm)])
    bvbc_done = bvev

    def emit_ln_chain(xtile, statc, lnout, ready):
        c_sum = statc[:, 0:1]; c_sq = statc[:, 1:2]; c_mean = statc[:, 2:3]
        c_m2 = statc[:, 3:4]; c_var = statc[:, 4:5]; c_lnv = statc[:, 5:6]
        c_rstd = statc[:, 6:7]; c_nmr = statc[:, 7:8]
        m_sum = op("scalar", lambda E: E.activation(scrbf[:, :], xtile[:, :], AF.Identity,
                                                   accum_out=c_sum), inc=sS, waits=ready)
        m_sq = op("vector", lambda E: E.scalar_tensor_tensor(
            out=scr32[:, :], in0=xtile[:, :], scalar=1.0, in1=xtile[:, :],
            op0=ALU.mult, op1=ALU.mult, accum_out=c_sq), inc=sV, waits=ready)
        m_mean = op("vector", lambda E: E.tensor_scalar(
            out=c_mean, in0=c_sum, scalar1=1.0 / D, scalar2=None, op0=ALU.mult),
            inc=sV, waits=[(sS, m_sum)])
        m_m2 = op("vector", lambda E: E.tensor_tensor(c_m2, c_mean, c_mean, ALU.mult),
                  inc=sV, waits=[(sV, m_mean)])
        c_ve = statc[:, 8:9]; c_lh = statc[:, 9:10]
        m_var = op("vector", lambda E: E.scalar_tensor_tensor(
            out=c_var, in0=c_sq, scalar=1.0 / D, in1=c_m2, op0=ALU.mult, op1=ALU.subtract),
            inc=sV, waits=[(sV, m_m2), (sV, m_sq)])
        m_ve = op("vector", lambda E: E.tensor_scalar(
            out=c_ve, in0=c_var, scalar1=1e-5, scalar2=None, op0=ALU.add),
            inc=sV, waits=[(sV, m_var)])
        m_lnv = op("scalar", lambda E: E.activation(c_lnv, c_ve, AF.Ln),
                   inc=sS, waits=[(sV, m_ve)])
        m_lh = op("vector", lambda E: E.tensor_scalar(
            out=c_lh, in0=c_lnv, scalar1=-0.5, scalar2=None, op0=ALU.mult),
            inc=sV, waits=[(sS, m_lnv)])
        m_rstd = op("scalar", lambda E: E.activation(c_rstd, c_lh, AF.Exp),
                    inc=sS, waits=[(sV, m_lh)])
        m_nmr = op("vector", lambda E: E.tensor_scalar(
            out=c_nmr, in0=c_mean, scalar1=c_rstd, scalar2=-1.0, op0=ALU.mult, op1=ALU.mult),
            inc=sV, waits=[(sS, m_rstd)])
        m_norm = op("scalar", lambda E: E.activation(
            lnout[:, :], xtile[:, :], AF.Identity, scale=c_rstd, bias=c_nmr),
            inc=sS, waits=[(sV, m_nmr)])
        return m_norm

    trwar = {0: None, 1: None}

    def emit_transposes(lnout, tt, norm_mark, dst, pses):
        last = None
        for c in range(NKC):
            pp = pses[c % 2]
            w = [(sS, norm_mark)]
            if trwar[c % 2] is not None:
                w.append(trwar[c % 2])
            mark = op("tensor", lambda E, pp=pp, c=c, lnout=lnout: E.transpose(
                pp[:, :], lnout[:, c * 128:(c + 1) * 128], ident_sb[:, :]),
                inc=sP, waits=w)
            em = op("vector", lambda E, pp=pp, c=c, tt=tt, dst=dst: E.tensor_copy(
                dst[:, c * TOK + tt * 128: c * TOK + (tt + 1) * 128], pp[:, :]),
                inc=sV, waits=[(sP, mark)])
            trwar[c % 2] = (sV, em)
            last = em
        return last

    ln_trans_done = {}
    for tt in range(NT):
        buf = tt % 2
        w = []
        if tt >= 2:
            w.append((sV, ln_trans_done[tt - 2]))
        xm = dma("sync", sX[buf], xt[buf][:, :], x_in[tt * 128:(tt + 1) * 128, :], waits=w)
        nm = emit_ln_chain(xt[buf], stat[:, buf * 10:buf * 10 + 10], ln_bf[buf], [(sX[buf], xm)])
        tm = emit_transposes(ln_bf[buf], tt, nm, lnxT, ptr_ps)
        ln_trans_done[tt] = tm
    ln1_done = ln_trans_done[NT - 1]

    if dbg == "ln1":
        d_ln = nc.dram_tensor("dbg_ln1", [128, NKC * TOK], BF16, kind="ExternalOutput").ap()
        dbg_outs["ln1"] = "dbg_ln1"
        dma("scalar", sSt[0], d_ln, lnxT[:, :], waits=[(sV, ln1_done)])
        barrier(); p.flush()
        return nc, dbg_outs

    barrier()
    p.flush()
    es1.close()

    es2 = ExitStack()
    wstrip = [sb("wstrip%d" % i, [128, 16 * 512], BF16, es2) for i in range(2)]
    evq = [sb("evq%d" % i, [128, 512], BF16, es2) for i in range(4)]
    wtmp = [sb("wtmp%d" % i, [128, II], BF16, es2) for i in range(2)]
    perm = [sb("perm%d" % i, [128, II], BF16, es2) for i in range(2)]
    psA = ps_("psA", [128, 4 * 512], F32, es2)
    psB = ps_("psB", [128, 4 * 512], F32, es2)

    wstrip_war = {0: None, 1: None}
    ps_war = {id(psA): None, id(psB): None}
    ev_war = {i: None for i in range(4)}

    def qk_proj(wsrc_bf, bias_sb, scale, dst_dram, cast_ready, st_sems):
        for dg in range(4):
            wbuf = wstrip[dg % 2]
            w = [cast_ready]
            if wstrip_war[dg % 2] is not None:
                w.append(wstrip_war[dg % 2])
            lm = 0
            for dk in range(NKC):
                lm = dmat("sync", sW[dg % 2], wbuf[:, dk * 512:(dk + 1) * 512],
                          wsrc_bf[dg * 512:(dg + 1) * 512, dk * 128:(dk + 1) * 128],
                          waits=w if dk == 0 else ())
            for dci in range(4):
                dc = dg * 4 + dci
                ps = psA if dc % 2 == 0 else psB
                wp = [(sW[dg % 2], lm)]
                if ps_war[id(ps)] is not None:
                    wp.append(ps_war[id(ps)])
                mmark = None
                for dk in range(NKC):
                    for tb in range(4):
                        last = (dk == NKC - 1 and tb == 3)
                        mmark = op("tensor", lambda E, dk=dk, tb=tb, dci=dci, ps=ps, wbuf=wbuf: E.matmul(
                            ps[:, tb * 512:(tb + 1) * 512],
                            wbuf[:, dk * 512 + dci * 128: dk * 512 + (dci + 1) * 128],
                            lnxT[:, dk * TOK + tb * 512: dk * TOK + (tb + 1) * 512],
                            start=(dk == 0), stop=(dk == NKC - 1)),
                            inc=sP if last else None,
                            waits=wp if (dk == 0 and tb == 0) else ())
                wstrip_war[dg % 2] = (sP, mmark)
                ev_last = None
                for tb in range(4):
                    eb = evq[tb]
                    wv_ = [(sP, mmark)]
                    if ev_war[tb] is not None:
                        wv_.append(ev_war[tb])
                    em = op("vector", lambda E, tb=tb, ps=ps, eb=eb, dc=dc: E.tensor_scalar(
                        out=eb[:, :], in0=ps[:, tb * 512:(tb + 1) * 512],
                        scalar1=bias_sb[:, dc:dc + 1], scalar2=scale,
                        op0=ALU.add, op1=ALU.mult), inc=sV, waits=wv_)
                    sm = dma("scalar", st_sems[tb], dst_dram[dc * 128:(dc + 1) * 128, tb * 512:(tb + 1) * 512],
                             eb[:, :], waits=[(sV, em)])
                    ev_war[tb] = (st_sems[tb], sm)
                    ev_last = em
                ps_war[id(ps)] = (sV, ev_last)

    qk_proj(wk_bf, bk_sb, SCALE, kv_loc[0:D, :], cast_mark["wk"], sSt)

    for vb in range(4):
        vstrip = wstrip[vb % 2]
        w = [cast_mark["wv"]]
        if wstrip_war[vb % 2] is not None:
            w.append(wstrip_war[vb % 2])
        lm = 0
        for dk in range(NKC):
            lm = dmat("sync", sW[vb % 2], vstrip[:, dk * 512:(dk + 1) * 512],
                      wv_bf[vb * 512:(vb + 1) * 512, dk * 128:(dk + 1) * 128],
                      waits=w if dk == 0 else ())
        for tt2 in range(8):
            ps = psA if tt2 % 2 == 0 else psB
            wp = [(sW[vb % 2], lm)]
            if ps_war[id(ps)] is not None:
                wp.append(ps_war[id(ps)])
            mmark = None
            for sub in range(2):
                tt = tt2 * 2 + sub
                for dk in range(NKC):
                    last = (sub == 1 and dk == NKC - 1)
                    mmark = op("tensor", lambda E, dk=dk, tt=tt, sub=sub, ps=ps, vstrip=vstrip: E.matmul(
                        ps[:, sub * 512:(sub + 1) * 512],
                        lnxT[:, dk * TOK + tt * 128: dk * TOK + (tt + 1) * 128],
                        vstrip[:, dk * 512:(dk + 1) * 512],
                        start=(dk == 0), stop=(dk == NKC - 1)),
                        inc=sP if last else None,
                        waits=wp if (sub == 0 and dk == 0) else ())
            wstrip_war[vb % 2] = (sP, mmark)
            ev_last = None
            for sub in range(2):
                tt = tt2 * 2 + sub
                eb = evq[sub]
                wv_ = [(sP, mmark), (sV, bvbc_done)]
                if ev_war[sub] is not None:
                    wv_.append(ev_war[sub])
                em = op("vector", lambda E, sub=sub, ps=ps, eb=eb, vb=vb: E.tensor_tensor(
                    eb[:, :], ps[:, sub * 512:(sub + 1) * 512],
                    bv_bc[:, vb * 512:(vb + 1) * 512], ALU.add), inc=sV, waits=wv_)
                sm = dma("scalar", sSt[sub], kv_loc[D + tt * 128: D + (tt + 1) * 128,
                                              vb * 512:(vb + 1) * 512],
                         eb[:, :], waits=[(sV, em)])
                ev_war[sub] = (sSt[sub], sm)
                ev_last = em
            ps_war[id(ps)] = (sV, ev_last)
    kv_store_marks = [(s, s.n) for s in sSt]

    op("gpsimd", lambda E: E.collective_compute(
        "AllGather", ALU.bypass, replica_groups=[[0, 1, 2, 3], [4, 5, 6, 7]],
        ins=[kv_loc], outs=[kv_g]), inc=sC1, waits=kv_store_marks)

    qk_proj(wq_bf, bq_sb, SCALE, q_loc, cast_mark["wq"], sQs)
    q_store_marks = [(s, s.n) for s in sQs]

    op("gpsimd", lambda E: E.collective_compute(
        "AllGather", ALU.bypass, replica_groups=[[0, 1, 2, 3], [4, 5, 6, 7]],
        ins=[q_loc], outs=[q_g]), inc=sC2, waits=q_store_marks)
    ag_q = sC2.n

    prep_war = {0: None, 1: None}
    for mi, (src, dst) in enumerate(((wg, wg_p), (wu, wu_p))):
        for dk in range(NKC):
            buf = (mi * NKC + dk) % 2
            w = []
            if prep_war[buf] is not None:
                w.append(prep_war[buf])
            lm = dma("gpsimd", sPw[buf], wtmp[buf][:, :], src[dk * 128:(dk + 1) * 128, :], waits=w)
            view = wtmp[buf][:, :].rearrange("p (g four) -> p g four", four=4)
            pm = None
            for j in range(4):
                pm = op("vector", lambda E, j=j, buf=buf, view=view: E.tensor_copy(
                    perm[buf][:, j * 2048:(j + 1) * 2048], view[:, :, j]),
                    inc=sV if j == 3 else None,
                    waits=[(sPw[buf], lm)] if j == 0 else ())
            sm = dma("scalar", sPst[buf], dst[dk * 128:(dk + 1) * 128, :], perm[buf][:, :],
                     waits=[(sV, pm)])
            prep_war[buf] = (sPst[buf], sm)
    ffnprep_stores = [(s, s.n) for s in sPst]

    if dbg == "qkv":
        barrier()
        dump_dram("kv_loc", kv_loc, [2 * D, TOK], BF16)
        dump_dram("q_loc", q_loc, [D, TOK], BF16)
        dump_dram("kv_g", kv_g, [8 * D, TOK], BF16)
        dump_dram("wg_p", wg_p, [D, II], BF16)
        dump_dram("wd_bf", wd_bf, [II, D], BF16)
        barrier(); p.flush()
        return nc, dbg_outs

    barrier()
    p.flush()
    es2.close()

    es3 = ExitStack()
    qbuf = [sb("qbuf%d" % i, [128, 1024], BF16, es3) for i in range(2)]
    kbuf = [sb("kbuf%d" % i, [128, 128], BF16, es3) for i in range(4)]
    vbuf = [sb("vbuf%d" % i, [128, 128], BF16, es3) for i in range(4)]
    pbuf = [sb("pbuf%d" % i, [128, 1024], BF16, es3) for i in range(3)]
    accs = sb("accs", [128, 1024], F32, es3)
    rrow = sb("rrow", [1, 1024], F32, es3)
    recbc = sb("recbc", [128, 1024], F32, es3)
    evat = [sb("evat%d" % i, [128, 512], BF16, es3) for i in range(2)]
    sc_ps = [ps_("scps%d" % i, [128, 1024], F32, es3) for i in range(2)]
    at_ps = [ps_("atps%d" % i, [128, 512], F32, es3) for i in range(2)]
    aux_ps = ps_("auxps", [128, 1024], F32, es3)

    I8 = mybir.dt.int8
    kv_g8 = kv_g.bitcast(I8).tensor
    q_g8 = q_g.bitcast(I8).tensor
    k_my8 = k_my.bitcast(I8).tensor
    v_my8 = v_my.bitcast(I8).tensor
    q_my8 = q_my.bitcast(I8).tensor

    def emit_localize_kq(E):
        pid = E.partition_id()
        gr = E.snap(pid % 4, min_val=0, max_val=3)
        with E.register("loc_off") as lo:
            last = None
            for hl in range(HLOC):
                for r4 in range(4):
                    o_base = (hl * 64 + r4 * 16) * 128 * 256
                    E.reg_mul(lo, gr, 4 * 128 * TOK * 2)
                    E.reg_add(lo, lo, ((r4 * 2 * D) * TOK + hl * 128 * TOK) * 2)
                    in_ap = bass.AP(kv_g8, lo, [[256, 16], [TOK * 2, 128], [1, 256]])
                    out_ap = bass.AP(k_my8, o_base, [[128 * 256, 16], [256, 128], [1, 256]])
                    last = E.dma_start(out=out_ap, in_=in_ap)
                    last.then_inc(sLOC.h, 16)
                E.reg_mul(lo, gr, 4 * 128 * TOK * 2)
                E.reg_add(lo, lo, hl * 128 * TOK * 2)
                in_q = bass.AP(q_g8, lo, [[D * TOK * 2, 4], [TOK * 2, 128], [1, TOK * 2]])
                out_q = bass.AP(q_my8, hl * 128 * S * 2, [[TOK * 2, 4], [S * 2, 128], [1, TOK * 2]])
                last = E.dma_start(out=out_q, in_=in_q)
                last.then_inc(sLOC.h, 16)
        return last

    def emit_localize_v(E):
        pid = E.partition_id()
        gr = E.snap(pid % 4, min_val=0, max_val=3)
        with E.register("locv_off") as lo:
            last = None
            for hl in range(HLOC):
                for r4 in range(4):
                    o_base = (hl * 64 + r4 * 16) * 128 * 256
                    E.reg_mul(lo, gr, 4 * 128 * 2)
                    E.reg_add(lo, lo, ((r4 * 2 * D + D) * TOK + hl * 128) * 2)
                    in_ap = bass.AP(kv_g8, lo, [[128 * TOK * 2, 16], [TOK * 2, 128], [1, 256]])
                    out_ap = bass.AP(v_my8, o_base, [[128 * 256, 16], [256, 128], [1, 256]])
                    last = E.dma_start(out=out_ap, in_=in_ap)
                    last.then_inc(sLOC.h, 16)
        return last
    p.wait("sync", sC1, 1)
    p.wait("sync", sC2, 1)
    p.emit("sync", emit_localize_kq)
    p.wait("scalar", sC1, 1)
    p.emit("scalar", emit_localize_v)
    sLOC.n += 16 * (HLOC * 4 * 2 + HLOC)
    localize_done = sLOC.n

    def kv_tile_ap(t_dram, hl, kt):
        return bass.AP(t_dram.tensor, (hl * 64 + kt) * 128 * 128, [[128, 128], [1, 128]])

    war = {
        "kbuf": [None] * 4, "vbuf": [None] * 4, "qbuf": [None, None],
        "pbuf_pe": [None, None, None], "pbuf_dve": [None, None, None],
        "scbank": [None, None], "atps": [None, None],
        "acc": None, "evat": [None, None], "aux": None,
    }
    PF = 3
    first_attn = True
    for hl in range(HLOC):
        for pr in range(8):
            kn = 8 * pr + 8
            qa_last = 8 * pr + 3
            qpb = pr % 2
            wq_ = [(sLOC, localize_done)]
            if war["qbuf"][qpb] is not None:
                wq_.append(war["qbuf"][qpb])
            qld = dma("sync", sQb[qpb], qbuf[qpb][:, :],
                      bass.AP(q_my.tensor, (hl * 128) * S + pr * 1024, [[S, 128], [1, 1024]]),
                      waits=wq_)
            accw = [war["acc"]] if war["acc"] is not None else []
            accm = op("vector", lambda E: E.memset(accs[:, :], 0.0), inc=sV, waits=accw)
            summark = accm
            expmark = {}
            kvld = {}

            def kv_prefetch(k):
                kb = k % 4
                wk_ = [(sLOC, localize_done)]
                if war["kbuf"][kb] is not None:
                    wk_.append(war["kbuf"][kb])
                km = dma("sync", sKb[kb], kbuf[kb][:, :], kv_tile_ap(k_my, hl, k), waits=wk_)
                wv_ = []
                if war["vbuf"][kb] is not None:
                    wv_.append(war["vbuf"][kb])
                vm = dma("sync", sVb[kb], vbuf[kb][:, :], kv_tile_ap(v_my, hl, k), waits=wv_)
                kvld[k] = (km, vm)

            def emit_attnv(k):
                kb = k % 4
                pb = k % 3
                qa_act = k <= qa_last
                em = expmark[k]
                wat = [(sS, em), (sVb[kb], kvld[k][1])]
                if k == 0:
                    for i_ in range(2):
                        if war["atps"][i_] is not None:
                            wat.append(war["atps"][i_])
                if qa_act:
                    op("tensor", lambda E, kb=kb, pb=pb, k=k, pr=pr: E.matmul(
                        at_ps[0][:, :], vbuf[kb][:, :], pbuf[pb][:, 0:512],
                        start=(k == 0), stop=(k == 8 * pr + 3)), waits=wat)
                    wat = ()
                am = op("tensor", lambda E, kb=kb, pb=pb, k=k, kn=kn: E.matmul(
                    at_ps[1][:, :], vbuf[kb][:, :], pbuf[pb][:, 512:1024],
                    start=(k == 0), stop=(k == kn - 1)), inc=sP, waits=wat)
                war["vbuf"][kb] = (sP, am)
                war["pbuf_pe"][pb] = (sP, am)

            for k in range(min(PF, kn)):
                kv_prefetch(k)
            for k in range(kn):
                kb = k % 4
                pb = k % 3
                par = k % 2
                qa_act = k <= qa_last
                c0, c1 = (0, 1024) if qa_act else (512, 1024)
                first_attn = False
                wsc = [(sKb[kb], kvld[k][0]), (sQb[qpb], qld)]
                if war["scbank"][par] is not None:
                    wsc.append(war["scbank"][par])
                if qa_act:
                    op("tensor", lambda E, kb=kb, par=par, qpb=qpb: E.matmul(
                        sc_ps[par][:, 0:512], kbuf[kb][:, :], qbuf[qpb][:, 0:512],
                        start=True, stop=True), waits=wsc)
                    wsc = ()
                sm = op("tensor", lambda E, kb=kb, par=par, qpb=qpb: E.matmul(
                    sc_ps[par][:, 512:1024], kbuf[kb][:, :], qbuf[qpb][:, 512:1024],
                    start=True, stop=True), inc=sP, waits=wsc)
                war["kbuf"][kb] = (sP, sm)
                war["qbuf"][qpb] = (sP, sm)
                if k > 0:
                    emit_attnv(k - 1)
                if k + PF < kn:
                    kv_prefetch(k + PF)
                dmask = None
                if qa_act and k >= 8 * pr:
                    d, lo = k - 8 * pr, 0
                elif not qa_act:
                    d, lo = k - 8 * pr - 4, 512
                else:
                    d = None
                if d is not None:
                    dmask = op("vector", lambda E, par=par, d=d, lo=lo: E.scalar_tensor_tensor(
                        out=sc_ps[par][:, lo:lo + 512], in0=sc_ps[par][:, lo:lo + 512],
                        scalar=1.0, in1=masks[:, d * 512:(d + 1) * 512],
                        op0=ALU.mult, op1=ALU.add), inc=sV,
                        waits=[(sP, sm), (sGc, mask_done)])
                wex = [(sP, sm)]
                if dmask is not None:
                    wex.append((sV, dmask))
                if war["pbuf_pe"][pb] is not None:
                    wex.append(war["pbuf_pe"][pb])
                if war["pbuf_dve"][pb] is not None:
                    wex.append(war["pbuf_dve"][pb])
                em = op("scalar", lambda E, par=par, pb=pb, c0=c0, c1=c1: E.activation(
                    pbuf[pb][:, c0:c1], sc_ps[par][:, c0:c1], AF.Exp), inc=sS, waits=wex)
                expmark[k] = em
                war["scbank"][par] = (sS, em)
                smv = op("vector", lambda E, pb=pb, c0=c0, c1=c1: E.tensor_tensor(
                    accs[:, c0:c1], accs[:, c0:c1], pbuf[pb][:, c0:c1], ALU.add),
                    inc=sV, waits=[(sS, em)])
                summark = smv
                war["pbuf_dve"][pb] = (sV, smv)
            emit_attnv(kn - 1)
            wones = [(sV, summark)]
            if war["aux"] is not None:
                wones.append(war["aux"])
            op("tensor", lambda E: E.matmul(
                aux_ps[0:1, 0:512], ones_col[:, :], accs[:, 0:512], start=True, stop=True),
                waits=wones)
            om = op("tensor", lambda E: E.matmul(
                aux_ps[0:1, 512:1024], ones_col[:, :], accs[:, 512:1024], start=True, stop=True),
                inc=sP)
            war["acc"] = (sP, om)
            rm = op("vector", lambda E: E.reciprocal(rrow[0:1, :], aux_ps[0:1, :]),
                    inc=sV, waits=[(sP, om)])
            op("tensor", lambda E: E.matmul(
                aux_ps[:, 0:512], ones_row[0:1, :], rrow[0:1, 0:512], start=True, stop=True),
                waits=[(sV, rm)])
            bm = op("tensor", lambda E: E.matmul(
                aux_ps[:, 512:1024], ones_row[0:1, :], rrow[0:1, 512:1024], start=True, stop=True),
                inc=sP)
            bcm = op("vector", lambda E: E.tensor_copy(recbc[:, :], aux_ps[:, :]),
                     inc=sV, waits=[(sP, bm)])
            for qi in range(2):
                wev = [(sV, bcm), (sP, sP.n)]
                if war["evat"][qi] is not None:
                    wev.append(war["evat"][qi])
                evm = op("vector", lambda E, qi=qi: E.scalar_tensor_tensor(
                    out=evat[qi][:, :], in0=at_ps[qi][:, :], scalar=1.0,
                    in1=recbc[:, qi * 512:(qi + 1) * 512], op0=ALU.mult, op1=ALU.mult),
                    inc=sV, waits=wev)
                war["atps"][qi] = (sV, evm)
                qt = 2 * pr + qi
                smq = dma("scalar", sQs[qi], attn_loc[hl * 128:(hl + 1) * 128, qt * 512:(qt + 1) * 512],
                          evat[qi][:, :], waits=[(sV, evm)])
                war["evat"][qi] = (sQs[qi], smq)
                if qi == 1:
                    war["aux"] = (sV, evm)
    attn_store_marks = [(sQs[0], sQs[0].n), (sQs[1], sQs[1].n)]

    op("gpsimd", lambda E: E.collective_compute(
        "AllGather", ALU.bypass, replica_groups=[[0, 1, 2, 3], [4, 5, 6, 7]],
        ins=[attn_loc], outs=[attn_g]), inc=sC3, waits=attn_store_marks)
    ag_at = sC3.n

    if dbg == "attn":
        barrier()
        dump_dram("attn_loc", attn_loc, [HLOC * 128, S], BF16)
        dump_dram("attn_g", attn_g, [D, S], BF16)
        barrier(); p.flush()
        return nc, dbg_outs

    barrier()
    p.flush()
    es3.close()

    es4 = ExitStack()
    attnT = sb("attnT", [128, NKC * TOK], BF16, es4)
    wstripO = [sb("wstripO%d" % i, [128, 16 * 512], BF16, es4) for i in range(2)]
    evo = [sb("evo%d" % i, [128, 512], BF16, es4) for i in range(4)]
    psA4 = ps_("psA4", [128, 4 * 512], F32, es4)
    psB4 = ps_("psB4", [128, 4 * 512], F32, es4)

    I8 = mybir.dt.int8
    attn_g8 = attn_g.bitcast(I8).tensor
    attnT8 = attnT[:, :].bitcast(I8)

    def emit_atload(E):
        pid = E.partition_id()
        gr = E.snap(pid % 4, min_val=0, max_val=3)
        with E.register("at_off") as lo:
            last = None
            for fc in range(NKC):
                E.reg_mul(lo, gr, TOK * 2)
                E.reg_add(lo, lo, fc * 128 * S * 2)
                in_ap = bass.AP(attn_g8, lo, [[S * 2, 128], [1, TOK * 2]])
                out_ap = bass.AP(attnT8.tensor, attnT8.offset + fc * TOK * 2,
                                 [[NKC * TOK * 2, 128], [1, TOK * 2]])
                last = E.dma_start(out=out_ap, in_=in_ap)
                last.then_inc(sLOC.h, 16)
        return last
    p.wait("sync", sC3, ag_at)
    p.emit("sync", emit_atload)
    sLOC.n += 16 * NKC
    atload = sLOC.n

    wstrip_war = {0: None, 1: None}
    ps_war = {id(psA4): None, id(psB4): None}
    ev_war = {i: None for i in range(4)}
    for dg in range(4):
        wbuf = wstripO[dg % 2]
        w = [cast_mark["wo"]]
        if wstrip_war[dg % 2] is not None:
            w.append(wstrip_war[dg % 2])
        lm = 0
        for fc in range(NKC):
            lm = dmat("sync", sW[dg % 2], wbuf[:, fc * 512:(fc + 1) * 512],
                      wo_bf[dg * 512:(dg + 1) * 512, fc * 128:(fc + 1) * 128],
                      waits=w if fc == 0 else ())
        for dci in range(4):
            dc = dg * 4 + dci
            ps = psA4 if dc % 2 == 0 else psB4
            wp = [(sW[dg % 2], lm), (sLOC, atload)]
            if ps_war[id(ps)] is not None:
                wp.append(ps_war[id(ps)])
            mmark = None
            for fc in range(NKC):
                for tb in range(4):
                    last = (fc == NKC - 1 and tb == 3)
                    mmark = op("tensor", lambda E, fc=fc, tb=tb, dci=dci, ps=ps, wbuf=wbuf: E.matmul(
                        ps[:, tb * 512:(tb + 1) * 512],
                        wbuf[:, fc * 512 + dci * 128: fc * 512 + (dci + 1) * 128],
                        attnT[:, fc * TOK + tb * 512: fc * TOK + (tb + 1) * 512],
                        start=(fc == 0), stop=(fc == NKC - 1)),
                        inc=sP if last else None,
                        waits=wp if (fc == 0 and tb == 0) else ())
            wstrip_war[dg % 2] = (sP, mmark)
            ev_last = None
            for tb in range(4):
                eb = evo[tb]
                wv_ = [(sP, mmark)]
                if ev_war[tb] is not None:
                    wv_.append(ev_war[tb])
                em = op("vector", lambda E, tb=tb, ps=ps, eb=eb, dc=dc: E.tensor_scalar(
                    out=eb[:, :], in0=ps[:, tb * 512:(tb + 1) * 512],
                    scalar1=bo_sb[:, dc:dc + 1], scalar2=None, op0=ALU.add),
                    inc=sV, waits=wv_)
                sm = dma("scalar", sSt[tb], o_dram[dc * 128:(dc + 1) * 128, tb * 512:(tb + 1) * 512],
                         eb[:, :], waits=[(sV, em)])
                ev_war[tb] = (sSt[tb], sm)
                ev_last = em
            ps_war[id(ps)] = (sV, ev_last)
    o_store_marks = [(s, s.n) for s in sSt]

    es4.close()
    es4b = ExitStack()
    xt4 = [sb("xt4%d" % i, [128, D], F32, es4b) for i in range(2)]
    oT4 = [sb("oT4%d" % i, [128, D], BF16, es4b) for i in range(2)]
    x2t = [sb("x2t%d" % i, [128, D], F32, es4b) for i in range(2)]
    ln_bf4 = [sb("lnbf4%d" % i, [128, D], BF16, es4b) for i in range(2)]
    stat4 = sb("stat4", [128, 20], F32, es4b)
    scr324 = sb("scr324", [128, D], F32, es4b)
    scrbf4 = sb("scrbf4", [128, D], BF16, es4b)
    trp4 = [ps_("trp4%d" % i, [128, 128], BF16, es4b) for i in range(2)]

    def emit_ln_chain4(xtile, statc, lnout, ready):
        c_sum = statc[:, 0:1]; c_sq = statc[:, 1:2]; c_mean = statc[:, 2:3]
        c_m2 = statc[:, 3:4]; c_var = statc[:, 4:5]; c_lnv = statc[:, 5:6]
        c_rstd = statc[:, 6:7]; c_nmr = statc[:, 7:8]
        m_sum = op("scalar", lambda E: E.activation(scrbf4[:, :], xtile[:, :], AF.Identity,
                                                   accum_out=c_sum), inc=sS, waits=ready)
        m_sq = op("vector", lambda E: E.scalar_tensor_tensor(
            out=scr324[:, :], in0=xtile[:, :], scalar=1.0, in1=xtile[:, :],
            op0=ALU.mult, op1=ALU.mult, accum_out=c_sq), inc=sV, waits=ready)
        m_mean = op("vector", lambda E: E.tensor_scalar(
            out=c_mean, in0=c_sum, scalar1=1.0 / D, scalar2=None, op0=ALU.mult),
            inc=sV, waits=[(sS, m_sum)])
        m_m2 = op("vector", lambda E: E.tensor_tensor(c_m2, c_mean, c_mean, ALU.mult),
                  inc=sV, waits=[(sV, m_mean)])
        c_ve = statc[:, 8:9]; c_lh = statc[:, 9:10]
        m_var = op("vector", lambda E: E.scalar_tensor_tensor(
            out=c_var, in0=c_sq, scalar=1.0 / D, in1=c_m2, op0=ALU.mult, op1=ALU.subtract),
            inc=sV, waits=[(sV, m_m2), (sV, m_sq)])
        m_ve = op("vector", lambda E: E.tensor_scalar(
            out=c_ve, in0=c_var, scalar1=1e-5, scalar2=None, op0=ALU.add),
            inc=sV, waits=[(sV, m_var)])
        m_lnv = op("scalar", lambda E: E.activation(c_lnv, c_ve, AF.Ln),
                   inc=sS, waits=[(sV, m_ve)])
        m_lh = op("vector", lambda E: E.tensor_scalar(
            out=c_lh, in0=c_lnv, scalar1=-0.5, scalar2=None, op0=ALU.mult),
            inc=sV, waits=[(sS, m_lnv)])
        m_rstd = op("scalar", lambda E: E.activation(c_rstd, c_lh, AF.Exp),
                    inc=sS, waits=[(sV, m_lh)])
        m_nmr = op("vector", lambda E: E.tensor_scalar(
            out=c_nmr, in0=c_mean, scalar1=c_rstd, scalar2=-1.0, op0=ALU.mult, op1=ALU.mult),
            inc=sV, waits=[(sS, m_rstd)])
        m_norm = op("scalar", lambda E: E.activation(
            lnout[:, :], xtile[:, :], AF.Identity, scale=c_rstd, bias=c_nmr),
            inc=sS, waits=[(sV, m_nmr)])
        return m_norm

    trwar = {0: None, 1: None}
    prev4 = {}
    x2st_war = {0: None, 1: None}
    x2_stores = 0
    for tt in range(NT):
        buf = tt % 2
        w = list(o_store_marks) if tt < 2 else [prev4[tt - 2]]
        xm = dma("sync", sX[buf], xt4[buf][:, :], x_in[tt * 128:(tt + 1) * 128, :], waits=w)
        wo_ = list(o_store_marks)
        if tt >= 2:
            wo_.append(prev4[tt - 2])
        om = dmat("sync", sKb[buf], oT4[buf][:, :], o_dram[0:D, tt * 128:(tt + 1) * 128], waits=wo_)
        wx2 = [(sX[buf], xm), (sKb[buf], om)]
        if x2st_war[buf] is not None:
            wx2.append(x2st_war[buf])
        x2m = op("vector", lambda E, buf=buf: E.tensor_tensor(
            x2t[buf][:, :], xt4[buf][:, :], oT4[buf][:, :], ALU.add),
            inc=sV, waits=wx2)
        x2_stores = dma("scalar", sQs[buf], x2d[tt * 128:(tt + 1) * 128, :], x2t[buf][:, :],
                        waits=[(sV, x2m)])
        x2st_war[buf] = (sQs[buf], x2_stores)
        nm = emit_ln_chain4(x2t[buf], stat4[:, buf * 10:buf * 10 + 10], ln_bf4[buf], [(sV, x2m)])
        tm = emit_transposes(ln_bf4[buf], tt, nm, lnxT, trp4)
        prev4[tt] = (sV, tm)
    ln2_done = prev4[NT - 1][1]

    if dbg == "x2":
        barrier()
        dump_dram("x2d", x2d, [TOK, D], F32)
        dump_dram("o_dram", o_dram, [D, TOK], BF16)
        datt = nc.dram_tensor("dbg_attnT", [128, NKC * TOK], BF16, kind="ExternalOutput").ap()
        dma("gpsimd", sCd, datt, attnT[:, :])
        dump_dram("attn_g", attn_g, [D, S], BF16)
        barrier(); p.flush()
        return nc, dbg_outs

    barrier()
    p.flush()
    es4b.close()
    es4.close()

    es5 = ExitStack()
    gstat = [sb("gstat%d" % i, [128, 2048], BF16, es5) for i in range(2)]
    ustat = [sb("ustat%d" % i, [128, 2048], BF16, es5) for i in range(2)]
    silu_sb = sb("silu_sb", [128, TOK], BF16, es5)
    hq = [sb("hq%d" % i, [128, TOK], BF16, es5) for i in range(4)]
    tq = [sb("tq%d" % i, [128, TOK], BF16, es5) for i in range(4)]
    sp1 = sb("sp1", [128, TOK], BF16, es5)
    sp2 = sb("sp2", [128, TOK], BF16, es5)
    sp3 = sb("sp3", [128, TOK], BF16, es5)
    sp4 = sb("sp4", [128, TOK], BF16, es5)
    hsx = [sb("hsx%d" % i, [128, TOK], BF16, es5) for i in range(4)]
    gps = ps_("gps", [128, 4 * 512], F32, es5)
    ups = ps_("ups", [128, 4 * 512], F32, es5)

    wstat_war = {0: None, 1: None}
    hq_war = [None] * 4
    tq_war = [None] * 4
    hs_war = [None] * 4
    silu_war = None
    g_ev = None
    u_ev = None
    sp4_war = None
    for m in range(16):
        habs = [None] * 4
        for j in range(4):
            c = j * 16 + m
            bufp = (m * 4 + j) % 2
            w = list(ffnprep_stores)
            if wstat_war[bufp] is not None:
                w.append(wstat_war[bufp])
            in_g = bass.AP(wg_p.tensor, c * 128, [[II, 128], [128 * II, 16], [1, 128]])
            in_u = bass.AP(wu_p.tensor, c * 128, [[II, 128], [128 * II, 16], [1, 128]])
            gv = gstat[bufp][:, :].rearrange("p (dk f) -> p dk f", f=128)
            uv = ustat[bufp][:, :].rearrange("p (dk f) -> p dk f", f=128)
            dma("sync", sW[bufp], gv, in_g, waits=w)
            lm = dma("sync", sW[bufp], uv, in_u)
            wg_ = [(sW[bufp], lm)]
            if g_ev is not None:
                wg_.append((sS, g_ev))
            gm = None
            for dk in range(NKC):
                for tb in range(4):
                    last = (dk == NKC - 1 and tb == 3)
                    gm = op("tensor", lambda E, dk=dk, tb=tb, bufp=bufp: E.matmul(
                        gps[:, tb * 512:(tb + 1) * 512],
                        gstat[bufp][:, dk * 128:(dk + 1) * 128],
                        lnxT[:, dk * TOK + tb * 512: dk * TOK + (tb + 1) * 512],
                        start=(dk == 0), stop=(dk == NKC - 1)),
                        inc=sP if last else None,
                        waits=wg_ if (dk == 0 and tb == 0) else ())
            wu_ = []
            if u_ev is not None:
                wu_.append((sV, u_ev))
            um = None
            for dk in range(NKC):
                for tb in range(4):
                    last = (dk == NKC - 1 and tb == 3)
                    um = op("tensor", lambda E, dk=dk, tb=tb, bufp=bufp: E.matmul(
                        ups[:, tb * 512:(tb + 1) * 512],
                        ustat[bufp][:, dk * 128:(dk + 1) * 128],
                        lnxT[:, dk * TOK + tb * 512: dk * TOK + (tb + 1) * 512],
                        start=(dk == 0), stop=(dk == NKC - 1)),
                        inc=sP if last else None,
                        waits=wu_ if (dk == 0 and tb == 0) else ())
            wstat_war[bufp] = (sP, um)
            wsl = [(sP, gm)]
            if silu_war is not None:
                wsl.append(silu_war)
            g_ev = op("scalar", lambda E: E.activation(silu_sb[:, :], gps[:, :], AF.Silu),
                      inc=sS, waits=wsl)
            whp = [(sP, um), (sS, g_ev)]
            if hq_war[j] is not None:
                whp.append(hq_war[j])
            u_ev = op("vector", lambda E, j=j: E.scalar_tensor_tensor(
                out=hq[j][:, :], in0=ups[:, :], scalar=1.0, in1=silu_sb[:, :],
                op0=ALU.mult, op1=ALU.mult), inc=sV, waits=whp)
            silu_war = (sV, u_ev)
            wab = [(sV, u_ev)]
            if tq_war[j] is not None:
                wab.append(tq_war[j])
            habs[j] = op("scalar", lambda E, j=j: E.activation(
                tq[j][:, :], hq[j][:, :], AF.Abs), inc=sS, waits=wab)
        wsp = [(sS, habs[0]), (sS, habs[1]), (sS, habs[2]), (sS, habs[3])]
        op("vector", lambda E: E.tensor_tensor(sp1[:, :], tq[0][:, :], tq[1][:, :], ALU.max),
           waits=wsp)
        op("vector", lambda E: E.tensor_tensor(sp2[:, :], tq[0][:, :], tq[1][:, :], ALU.min))
        op("vector", lambda E: E.tensor_tensor(sp3[:, :], tq[2][:, :], tq[3][:, :], ALU.max))
        w1 = op("vector", lambda E: E.tensor_tensor(sp4[:, :], tq[2][:, :], tq[3][:, :], ALU.min),
                inc=sV)
        op("vector", lambda E: E.tensor_tensor(sp1[:, :], sp1[:, :], sp3[:, :], ALU.min),
           waits=[(sV, w1)])
        w2 = op("vector", lambda E: E.tensor_tensor(sp2[:, :], sp2[:, :], sp4[:, :], ALU.max),
                inc=sV)
        w3 = op("vector", lambda E: E.tensor_tensor(sp3[:, :], sp1[:, :], sp2[:, :], ALU.max),
                inc=sV, waits=[(sV, w2)])
        for j in range(4):
            c = j * 16 + m
            wj = [(sV, w3)]
            if hs_war[j] is not None:
                wj.append(hs_war[j])
            if sp4_war is not None:
                wj.append(sp4_war)
            w4 = op("vector", lambda E, j=j: E.tensor_tensor(
                sp4[:, :], tq[j][:, :], sp3[:, :], ALU.is_ge), inc=sV, waits=wj)
            w5 = op("vector", lambda E, j=j: E.tensor_tensor(
                hsx[j][:, :], sp4[:, :], hq[j][:, :], ALU.mult), inc=sV, waits=[(sV, w4)])
            sp4_war = (sV, w5)
            hq_war[j] = (sV, w5)
            tq_war[j] = (sV, w4)
            hs_war[j] = (sSt[j], dma("scalar", sSt[j], hsT[c * 128:(c + 1) * 128, :], hsx[j][:, :],
                                 waits=[(sV, w5)]))
    hsT_stores = [(s, s.n) for s in sSt]

    if dbg == "gateup":
        barrier()
        dump_dram("hsT", hsT, [II, TOK], BF16)
        barrier(); p.flush()
        return nc, dbg_outs

    barrier()
    p.flush()
    es5.close()
    es_ln.close()

    es6 = ExitStack()
    hblk = sb("hblk", [128, NIC * 1024], BF16, es6)
    wdt = [sb("wdt%d" % i, [128, 256], BF16, es6) for i in range(4)]
    evd = [sb("evd%d" % i, [128, 512], BF16, es6) for i in range(2)]
    dps = ps_("dps", [128, 4 * 512], F32, es6)
    xt6 = [sb("xt6%d" % i, [128, D], F32, es6) for i in range(2)]
    dT6 = [sb("dT6%d" % i, [128, D], BF16, es6) for i in range(2)]
    x3t = [sb("x3t%d" % i, [128, D], F32, es6) for i in range(2)]

    dn_stores = 0
    out_war = {0: None, 1: None}
    x3_war = {0: None, 1: None}
    blk_mm_done = None
    for b in range(2):
        hbl = 0
        for c in range(NIC):
            w = []
            if c == 0:
                w.extend(hsT_stores)
                if blk_mm_done is not None:
                    w.append(blk_mm_done)
            hbl = dma("sync", sLOC, hblk[:, c * 1024:(c + 1) * 1024],
                      hsT[c * 128:(c + 1) * 128, b * 1024:(b + 1) * 1024], waits=w)
        wdt_war = {i: None for i in range(4)}
        ps_ev6 = None
        ev6_war = {0: None, 1: None}
        for dcp in range(8):
            wdl_marks = {}
            mmark = None

            def wd_prefetch(c, dcp=dcp, b=b):
                j, mm_ = c // 16, c % 16
                bufw = c % 4
                in_wd = bass.AP(wd_bf.tensor, (512 * mm_ + j) * D + dcp * 256,
                                [[4 * D, 128], [1, 256]])
                w = [cast_mark["wd"]] if (b == 0 and dcp == 0 and c < 4) else []
                if wdt_war[bufw] is not None:
                    w.append(wdt_war[bufw])
                wdl_marks[c] = dma("sync", sWd[bufw], wdt[bufw][:, :], in_wd, waits=w)

            for c in range(3):
                wd_prefetch(c)
            for c in range(NIC):
                bufw = c % 4
                if c + 3 < NIC:
                    wd_prefetch(c + 3)
                mmw = [(sWd[bufw], wdl_marks[c]), (sLOC, hbl)]
                if c == 0 and ps_ev6 is not None:
                    mmw.append(ps_ev6)
                for dci in range(2):
                    for sub in range(2):
                        last = (dci == 1 and sub == 1)
                        mmark = op("tensor", lambda E, c=c, dci=dci, sub=sub, bufw=bufw: E.matmul(
                            dps[:, (dci * 2 + sub) * 512:(dci * 2 + sub + 1) * 512],
                            wdt[bufw][:, dci * 128:(dci + 1) * 128],
                            hblk[:, c * 1024 + sub * 512: c * 1024 + (sub + 1) * 512],
                            start=(c == 0), stop=(c == NIC - 1)),
                            inc=sP if last else None,
                            waits=mmw if (dci == 0 and sub == 0) else ())
                wdt_war[bufw] = (sP, mmark)
            blk_mm_done = (sP, mmark)
            ev_last = None
            for dci in range(2):
                dc = dcp * 2 + dci
                for sub in range(2):
                    eb = evd[sub]
                    wv_ = [(sP, mmark)]
                    if ev6_war[sub] is not None:
                        wv_.append(ev6_war[sub])
                    em = op("vector", lambda E, dci=dci, sub=sub, eb=eb: E.tensor_copy(
                        eb[:, :], dps[:, (dci * 2 + sub) * 512:(dci * 2 + sub + 1) * 512]),
                        inc=sV, waits=wv_)
                    sm6 = dma("scalar", sSt[sub], dn_dram[dc * 128:(dc + 1) * 128,
                                              b * 1024 + sub * 512: b * 1024 + (sub + 1) * 512],
                        eb[:, :], waits=[(sV, em)])
                    ev6_war[sub] = (sSt[sub], sm6)
                    ev_last = em
            ps_ev6 = (sV, ev_last)
        dn_stores = [(sSt[0], sSt[0].n), (sSt[1], sSt[1].n)]
        for ttl in range(8):
            tt = b * 8 + ttl
            buf = ttl % 2
            w = []
            if x3_war[buf] is not None:
                w.append(x3_war[buf])
            xm = dma("sync", sX[buf], xt6[buf][:, :], x2d[tt * 128:(tt + 1) * 128, :], waits=w)
            dm_ = dmat("sync", sKb[buf], dT6[buf][:, :],
                       dn_dram[0:D, tt * 128: tt * 128 + 128],
                       waits=list(dn_stores))
            wx3 = [(sX[buf], xm), (sKb[buf], dm_)]
            if out_war[buf] is not None:
                wx3.append(out_war[buf])
            x3m = op("vector", lambda E, buf=buf: E.tensor_tensor(
                x3t[buf][:, :], xt6[buf][:, :], dT6[buf][:, :], ALU.add),
                inc=sV, waits=wx3)
            x3_war[buf] = (sV, x3m)
            om_ = dma("scalar", sSt[2 + buf], out[tt * 128:(tt + 1) * 128, :], x3t[buf][:, :],
                      waits=[(sV, x3m)])
            out_war[buf] = (sSt[2 + buf], om_)

    barrier()
    p.flush()
    es6.close()
    es.close()
    return nc, dbg_outs



def _core_inputs(inputs, core):
    X = inputs["L0_attn_X"]
    b, g = core // 4, core % 4
    xs = np.ascontiguousarray(X[b, g * TOK:(g + 1) * TOK, :], dtype=np.float32)
    def b2(v):
        return np.ascontiguousarray(np.asarray(v, np.float32).reshape(NKC, 128).T)
    return {
        "x": xs,
        "wq": np.ascontiguousarray(inputs["L0_attn_Wq"], np.float32),
        "wk": np.ascontiguousarray(inputs["L0_attn_Wk"], np.float32),
        "wv": np.ascontiguousarray(inputs["L0_attn_Wv"], np.float32),
        "wo": np.ascontiguousarray(inputs["L0_attn_Wo"], np.float32),
        "bq2": b2(inputs["L0_attn_bq"]),
        "bk2": b2(inputs["L0_attn_bk"]),
        "bo2": b2(inputs["L0_attn_bo"]),
        "bv_row": np.ascontiguousarray(np.asarray(inputs["L0_attn_bv"], np.float32).reshape(1, D)),
        "wg": np.ascontiguousarray(inputs["L0_ffn_w_gate"], np.float32),
        "wu": np.ascontiguousarray(inputs["L0_ffn_w_up"], np.float32),
        "wd": np.ascontiguousarray(inputs["L0_ffn_w_down"], np.float32),
    }


def kernel(**inputs):
    nc, _ = build()
    in_maps = [_core_inputs(inputs, c) for c in range(8)]
    res = run_bass_kernel_spmd(nc, in_maps, core_ids=list(range(8)))
    Y = np.empty((B, S, D), np.float32)
    for c in range(8):
        b, g = c // 4, c % 4
        Y[b, g * TOK:(g + 1) * TOK, :] = res.results[c]["out"]
    return Y
